# version 08
# baseline (speedup 1.0000x reference)
import numpy as np
import ml_dtypes
import concourse.bass as bass
import concourse.mybir as mybir
from concourse.bass import IndirectOffsetOnAxis
from concourse.bass_utils import run_bass_kernel_spmd

F32 = mybir.dt.float32
BF16 = mybir.dt.bfloat16
U32 = mybir.dt.uint32
AF = mybir.ActivationFunctionType
ALU = mybir.AluOpType
AX = mybir.AxisListType
BF = ml_dtypes.bfloat16

NCORES = 8
T = 4096
D = 1024
ST = 1024
NST = T // ST
NTILE = ST // 128
LN_EPS = 1e-5

COMPUTE_SEM = {"tensor": "cT", "vector": "cV", "scalar": "cS", "gpsimd": "cG"}
DMA_SEM = {"sync": "dS", "gpsimd": "dG", "scalar": "dC"}


class Rec:
    def __init__(self, nc):
        self.nc = nc
        self.stream = {e: [] for e in ("tensor", "vector", "scalar", "gpsimd", "sync")}
        self.cnt = {s: 0 for s in ("cT", "cV", "cS", "cG", "dS", "dG", "dC")}
        self.lastwait = {e: {} for e in self.stream}

    def wait(self, eng, evs):
        if evs is None:
            return
        if isinstance(evs, tuple) and len(evs) == 2 and isinstance(evs[0], str):
            evs = [evs]
        own = COMPUTE_SEM.get(eng)
        for ev in evs:
            if ev is None:
                continue
            sem, val = ev
            if sem == own:
                continue
            if self.lastwait[eng].get(sem, -1) >= val:
                continue
            self.lastwait[eng][sem] = val
            self.stream[eng].append(("w", sem, val))

    def op(self, eng, fn, deps=None, sig=True, drain=False):
        self.wait(eng, deps)
        if drain:
            self.stream[eng].append(("o", fn, None, 0))
            if sig:
                sem = COMPUTE_SEM[eng]
                self.cnt[sem] += 1
                self.stream[eng].append(("o", lambda e: e.drain(), sem, 1))
                return (sem, self.cnt[sem])
            self.stream[eng].append(("o", lambda e: e.drain(), None, 0))
            return None
        if sig:
            sem = COMPUTE_SEM[eng]
            self.cnt[sem] += 1
            self.stream[eng].append(("o", fn, sem, 1))
            return (sem, self.cnt[sem])
        self.stream[eng].append(("o", fn, None, 0))
        return None

    def dma(self, eng, fn, deps=None):
        self.wait(eng, deps)
        sem = DMA_SEM[eng]
        self.cnt[sem] += 16
        self.stream[eng].append(("o", fn, sem, 16))
        return (sem, self.cnt[sem])

    def snap(self):
        return [(s, v) for s, v in self.cnt.items() if v > 0]

    def emit(self):
        nc = self.nc
        sems = {name: nc.alloc_semaphore(name) for name in self.cnt}
        with nc.Block() as block:
            def runner(name):
                def run(eng):
                    for item in self.stream[name]:
                        if item[0] == "w":
                            eng.wait_ge(sems[item[1]], item[2])
                        else:
                            inst = item[1](eng)
                            if item[2] is not None:
                                inst.then_inc(sems[item[2]], item[3])
                return run
            block.tensor(runner("tensor"))
            block.vector(runner("vector"))
            block.scalar(runner("scalar"))
            block.gpsimd(runner("gpsimd"))
            block.sync(runner("sync"))


def bk(ps, b):
    return ps[0:128, b * 512:(b + 1) * 512]


def build_program(dbg=False):
    nc = bass.Bass("TRN2", target_bir_lowering=False, detect_race_conditions=False)
    r = Rec(nc)

    dx = nc.dram_tensor("x_rows", [T, D], F32, kind="ExternalInput")
    d_pl = nc.dram_tensor("w_pl", [D, D], F32, kind="ExternalInput")
    d_pg = nc.dram_tensor("w_pg", [D, D], F32, kind="ExternalInput")
    d_wlin = nc.dram_tensor("w_l_inT", [D, 3 * D], F32, kind="ExternalInput")
    d_wgin = nc.dram_tensor("w_g_inT", [D, 3 * D], F32, kind="ExternalInput")
    d_blin = nc.dram_tensor("b_l_in3", [128, 24], F32, kind="ExternalInput")
    d_bgin = nc.dram_tensor("b_g_in3", [128, 24], F32, kind="ExternalInput")
    d_wlout = nc.dram_tensor("w_l_outT", [D, D], F32, kind="ExternalInput")
    d_wgout = nc.dram_tensor("w_g_outT", [D, D], F32, kind="ExternalInput")
    d_blout = nc.dram_tensor("b_l_out8", [128, 8], F32, kind="ExternalInput")
    d_bgout = nc.dram_tensor("b_g_out8", [128, 8], F32, kind="ExternalInput")
    d_fw = nc.dram_tensor("fusion_w", [1, 2], F32, kind="ExternalInput")
    d_gam = nc.dram_tensor("gamma_b", [128, D], F32, kind="ExternalInput")
    d_bet = nc.dram_tensor("beta_b", [128, D], F32, kind="ExternalInput")
    d_wq = nc.dram_tensor("wq", [8 * D, D], F32, kind="ExternalInput")
    d_keysT = nc.dram_tensor("keysT", [16, 512, 128], F32, kind="ExternalInput")
    d_win = nc.dram_tensor("w_inT", [D, 2 * D], F32, kind="ExternalInput")
    d_wout = nc.dram_tensor("w_outT", [2 * D, D], F32, kind="ExternalInput")
    d_ina = nc.dram_tensor("tab_ina", [16384, D], F32, kind="ExternalInput")
    d_inb = nc.dram_tensor("tab_inb", [16384, 2 * D], F32, kind="ExternalInput")
    d_outa = nc.dram_tensor("tab_outa", [16384, 2 * D], F32, kind="ExternalInput")
    d_outb = nc.dram_tensor("tab_outb", [16384, D], F32, kind="ExternalInput")
    d_ident = nc.dram_tensor("ident_bf", [128, 128], BF16, kind="ExternalInput")
    d_tril = nc.dram_tensor("tril8_bf", [128, 128], BF16, kind="ExternalInput")
    d_ones = nc.dram_tensor("onesrow", [1, 128], F32, kind="ExternalInput")
    d_onesc = nc.dram_tensor("onescol", [128, 1], BF16, kind="ExternalInput")
    d_onesm = nc.dram_tensor("onesmat", [128, 128], BF16, kind="ExternalInput")
    d_onesr_bf = nc.dram_tensor("onesrow_bf", [1, 128], BF16, kind="ExternalInput")
    d_zc = nc.dram_tensor("zerocol", [128, 128], BF16, kind="ExternalInput")
    d_cv = nc.dram_tensor("constv", [128, 4], F32, kind="ExternalInput")

    d_out = nc.dram_tensor("out_rows", [T, D], F32, kind="ExternalOutput")

    KD = "ExternalOutput" if dbg else "Internal"
    s_mt = nc.dram_tensor("mt_scr", [128, 8, 2 * D], BF16, kind=KD)
    s_wlo = nc.dram_tensor("wlo_scr", [128, 8, D], BF16, kind=KD)
    s_wgo = nc.dram_tensor("wgo_scr", [128, 8, D], BF16, kind=KD)
    s_xntm = nc.dram_tensor("xntm_scr", [128, NTILE, D], BF16, kind=KD)
    s_stm = nc.dram_tensor("stm_scr", [128, NTILE, D], BF16, kind=KD)
    s_att = nc.dram_tensor("att_scr", [2, 8, 128, ST], BF16, kind=KD)
    if dbg:
        s_hid = nc.dram_tensor("hid_dbg", [128, NTILE, 2 * D], BF16, kind=KD)
        s_edbg = nc.dram_tensor("e_dbg", [128, NTILE, 8], U32, kind=KD)
        s_ln1 = nc.dram_tensor("ln1_dbg", [128, 5, NTILE], F32, kind=KD)
        s_gdbg = nc.dram_tensor("g_dbg", [128, NTILE, 8], F32, kind=KD)
        s_mxdbg = nc.dram_tensor("mx_dbg", [128, 16, 8], F32, kind=KD)

    A = nc.alloc_sbuf_tensor
    ident = A("ident", [128, 128], BF16)
    tril8 = A("tril8", [128, 128], BF16)
    onesf = A("onesf", [1, 128], F32)
    onesc = A("onesc", [128, 1], BF16)
    onesm = A("onesm", [128, 128], BF16)
    onesr_bf = A("onesr_bf", [1, 128], BF16)
    zcol = A("zcol", [128, 128], BF16)
    cvec = A("cvec", [128, 4], F32)
    gam_b = A("gam_b", [128, D], F32)
    bet_b = A("bet_b", [128, D], F32)
    blin3 = A("blin3", [128, 24], F32)
    bgin3 = A("bgin3", [128, 24], F32)
    fw_sb = A("fw_sb", [1, 2], F32)
    fwtmp = A("fwtmp", [1, 4], F32)
    w01 = A("w01", [1, 2], F32)
    w01b = A("w01b", [128, 2], F32)
    fb8 = A("fb8", [128, 8], F32)
    blo8 = A("blo8", [128, 8], F32)
    bgo8 = A("bgo8", [128, 8], F32)
    s1c = A("s1c", [128, NTILE], F32)
    s2c = A("s2c", [128, NTILE], F32)
    mu_c = A("mu_c", [128, NTILE], F32)
    rr_c = A("rr_c", [128, NTILE], F32)
    mr_c = A("mr_c", [128, NTILE], F32)
    ln_t1 = A("ln_t1", [128, NTILE], F32)

    xn_fm = A("xn_fm", [128, 8, ST], BF16)
    arena_w = A("arena_w", [128, 32768], BF16)
    w_proj = arena_w.reshape([128, 32, D])[:, 0:8, :]
    w_qkv = arena_w.reshape([128, 32, D])[:, 8:32, :]
    w_lo = arena_w.reshape([128, 32, D])[:, 0:8, :]
    w_go = arena_w.reshape([128, 32, D])[:, 8:16, :]
    w_win = arena_w.reshape([128, 16, 2 * D])[:, 0:8, :]
    mt_sb = arena_w.reshape([128, 16, 2 * D])[:, 8:16, :]
    w_wout = arena_w.reshape([128, 32, D])[:, 16:32, :]
    xtm = arena_w.bitcast(F32).reshape([128, 16, D])[:, 0:8, :]
    wq_stage = arena_w.reshape([128, 32, D])[:, 0:4, :]
    keys_stage = arena_w.reshape([128, 32 * D])[:, 4 * D:4 * D + 512]
    mt_stage = arena_w.reshape([128, 32, D])[:, 6:14, 0:128]
    wsc_stage = arena_w.bitcast(F32).reshape([128, 16, D])[:, 8:9, :]
    wsc_out = arena_w.reshape([128, 32, D])[:, 20:21, :]

    arena_x = A("arena_x", [128, 24576 // 2], BF16)
    lqs_fm = arena_x.reshape([128, 12, D])[:, 0:8, :]
    fused_fm = arena_x.reshape([128, 12, D])[:, 0:8, :]
    gbuf2 = arena_x.reshape([128, 4, 3072])[:, 0:2, :]

    hid_all = A("hid_all", [128, NTILE, 2 * D], BF16)
    q_sb = A("q_sb", [128, ST], BF16)
    k_sb = A("k_sb", [128, ST], BF16)
    vf_sb = A("vf_sb", [128, ST], BF16)
    v_tm = A("v_tm", [128, 8, 128], BF16)
    pT = A("pT", [128, ST], BF16)
    atile = A("atile", [128, 2, ST], BF16)
    sums64 = A("sums64", [128, ST], F32)
    astr = A("astr", [128, 2, 512], BF16)
    mx8 = A("mx8", [128, 16, 8], F32)
    mi_u = A("mi_u", [128, 16, 8], U32)
    score8 = A("score8", [128, 8], F32)
    gate_all = A("gate_all", [128, NTILE, 8], F32)
    e_all = A("e_all", [128, NTILE, 8], U32)
    stt_scr = A("stt_scr", [128, 3072], BF16)
    acc_f32 = A("acc_f32", [128, 2 * D], F32)
    xn_tmg = A("xn_tmg", [128, 2, D], BF16)
    s_tmg = A("s_tmg", [128, 2, D], BF16)
    hid_fm = A("hid_fm", [128, 16, 128], BF16)
    g_t = A("g_t", [128, D], BF16)
    x2_t = A("x2_t", [128, 2, D], F32)
    lingc = A("lingc", [128, 8], F32)

    ps = nc.alloc_psum_tensor("ps", [128, 4096], F32)

    rem = nc.sbuf_bytes_remaining
    assert rem >= 0, f"SBUF over budget by {-rem} bytes"

    ev = {}

    for dst, src in ((ident, d_ident), (tril8, d_tril), (onesf, d_ones),
                     (onesc, d_onesc), (onesm, d_onesm), (onesr_bf, d_onesr_bf), (zcol, d_zc), (cvec, d_cv),
                     (gam_b, d_gam), (bet_b, d_bet), (blin3, d_blin),
                     (bgin3, d_bgin), (fw_sb, d_fw), (blo8, d_blout),
                     (bgo8, d_bgout)):
        ec = r.dma("sync", lambda e, dst=dst, src=src: e.dma_start(out=dst[:], in_=src[:]))

    r.op("scalar", lambda e: e.preload_act_table(AF.Exp), sig=False)
    e1 = r.op("vector", lambda e: e.reduce_max(fwtmp[:, 0:1], fw_sb[:], axis=AX.X), deps=[ec], drain=True)
    e1 = r.op("vector", lambda e: e.tensor_scalar(fwtmp[:, 1:2], fwtmp[:, 0:1], -1.0, None, ALU.mult), deps=[e1], drain=True)
    e1 = r.op("scalar", lambda e: e.activation(w01[:], fw_sb[:], AF.Exp, bias=fwtmp[:, 1:2]), deps=[e1], drain=True)
    e1 = r.op("vector", lambda e: e.reduce_sum(fwtmp[:, 0:1], w01[:], axis=AX.X), deps=[e1], drain=True)
    e1 = r.op("vector", lambda e: e.reciprocal(fwtmp[:, 1:2], fwtmp[:, 0:1]), deps=[e1], drain=True)
    e1 = r.op("vector", lambda e: e.tensor_scalar(fwtmp[:, 2:4], w01[:], fwtmp[:, 1:2], None, ALU.mult), deps=[e1], drain=True)
    e1t = r.op("tensor", lambda e: e.matmul(ps[0:128, 2048:2050], onesf[:], fwtmp[:, 2:4], start=True, stop=True), deps=[e1])
    e_w01b = r.op("vector", lambda e: e.tensor_copy(w01b[:], ps[0:128, 2048:2050]), deps=[e1t], drain=True)
    ev["av_f"] = e_w01b
    e2 = r.op("vector", lambda e: e.tensor_scalar(fb8[:], blo8[:], w01b[:, 0:1], None, ALU.mult), deps=[e_w01b], drain=True)
    ev["fb8"] = r.op("vector", lambda e: e.scalar_tensor_tensor(
        out=fb8[:], in0=bgo8[:], scalar=w01b[:, 1:2], in1=fb8[:], op0=ALU.mult, op1=ALU.add), deps=[e2], drain=True)

    e_prev = None
    for (dsrc, dscr, wcol) in ((d_wlout, s_wlo, 0), (d_wgout, s_wgo, 1)):
        for kc in range(8):
            eld = r.dma("sync", lambda e, dsrc=dsrc, kc=kc: e.dma_start(
                out=wsc_stage[:, 0, :], in_=dsrc[kc * 128:(kc + 1) * 128, :]), deps=[e_prev])
            esc = r.op("scalar", lambda e, wcol=wcol: e.activation(
                wsc_out[:, 0, :], wsc_stage[:, 0, :], AF.Copy, scale=w01b[:, wcol:wcol + 1]),
                deps=[eld, e_w01b])
            e_prev = r.dma("sync", lambda e, dscr=dscr, kc=kc: e.dma_start(
                out=dscr[:, kc, :], in_=wsc_out[:, 0, :]), deps=[esc])
    ev["wscaled"] = e_prev

    e_sp = ev["wscaled"]
    for ph in range(16):
        ewq = r.dma("gpsimd", lambda e, ph=ph: e.dma_start(
            out=wq_stage[:], in_=d_wq[ph * 512:(ph + 1) * 512, :].rearrange("(k p) d -> p k d", p=128)),
            deps=[e_sp])
        eky = r.dma("gpsimd", lambda e, ph=ph: e.dma_start(
            out=keys_stage.rearrange("p (k d) -> p k d", k=4), in_=d_keysT[ph].rearrange("(k p) d -> p k d", p=128)), deps=[e_sp])
        emms = []
        for mc in range(8):
            b = mc % 4
            col = (mc // 4) * 128
            emm = None
            for kc in range(4):
                emm = r.op("tensor", lambda e, mc=mc, kc=kc, b=b, col=col: e.matmul(
                    bk(ps, b)[:, col:col + 128],
                    wq_stage[:, kc, mc * 128:(mc + 1) * 128],
                    keys_stage[:, kc * 128:(kc + 1) * 128],
                    start=(kc == 0), stop=(kc == 3)),
                    deps=[ewq, eky, ev.get(f"mtb{b}{mc // 4}")] if kc == 0 else None,
                    sig=(kc == 3))
            emms.append(emm)
        eevs = []
        for mc in range(8):
            b = mc % 4
            col = (mc // 4) * 128
            eev = r.op("vector", lambda e, mc=mc, b=b, col=col: e.tensor_copy(
                mt_stage[:, mc, :], bk(ps, b)[:, col:col + 128]), deps=[emms[mc], e_sp])
            ev[f"mtb{b}{mc // 4}"] = eev
            eevs.append(eev)
        e_sp = r.dma("sync", lambda e, ph=ph: e.dma_start(
            out=s_mt[:, :, ph * 128:(ph + 1) * 128], in_=mt_stage[:]), deps=eevs)
    ev["mt_done"] = e_sp

    e_out_final = None

    for st in range(NST):
        t0 = st * ST
        fence0 = r.snap()

        e_x = []
        for tb in range(NTILE):
            e_x.append(r.dma("sync", lambda e, tb=tb, t0=t0: e.dma_start(
                out=xtm[:, tb, :], in_=dx[t0 + tb * 128:t0 + tb * 128 + 128, :]), deps=fence0))
        e_st = []
        for tb in range(NTILE):
            r.op("scalar", lambda e, tb=tb: e.activation(
                stt_scr[:, 0:1024], xtm[:, tb, :], AF.Copy, accum_out=s1c[:, tb:tb + 1]),
                deps=[e_x[tb]] + fence0, sig=False)
            e_st.append(r.op("scalar", lambda e, tb=tb: e.activation(
                stt_scr[:, 1024:2048], xtm[:, tb, :], AF.Square, accum_out=s2c[:, tb:tb + 1]),
                drain=(tb == NTILE - 1)))
        em = r.op("vector", lambda e: e.tensor_scalar(mu_c[:], s1c[:], 1.0 / D, None, ALU.mult), deps=[e_st[-1]] + fence0, drain=True)
        em = r.op("vector", lambda e: e.tensor_copy(mr_c[:], mu_c[:]), deps=[em], drain=True)
        em = r.op("vector", lambda e: e.tensor_mul(ln_t1[:], mu_c[:], mr_c[:]), deps=[em], drain=True)
        em = r.op("vector", lambda e: e.tensor_scalar(s1c[:], s2c[:], 1.0 / D, None, ALU.mult), deps=[em], drain=True)
        em = r.op("vector", lambda e: e.tensor_sub(ln_t1[:], s1c[:], ln_t1[:]), deps=[em], drain=True)
        r.op("scalar", lambda e: e.preload_act_table(AF.Sqrt), sig=False)
        em = r.op("scalar", lambda e: e.activation(s2c[:], ln_t1[:], AF.Sqrt, bias=cvec[:, 0:1]), deps=[em], drain=True)
        em = r.op("vector", lambda e: e.reciprocal(rr_c[:], s2c[:]), deps=[em], drain=True)
        em = r.op("vector", lambda e: e.tensor_mul(mr_c[:], mu_c[:], rr_c[:]), deps=[em], drain=True)
        if dbg:
            r.dma("sync", lambda e: e.dma_start(out=s_ln1[:, 0, :], in_=s1c[:]), deps=[em])
            r.dma("sync", lambda e: e.dma_start(out=s_ln1[:, 1, :], in_=ln_t1[:]), deps=[em])
            r.dma("sync", lambda e: e.dma_start(out=s_ln1[:, 4, :], in_=s2c[:]), deps=[em])
            r.dma("sync", lambda e: e.dma_start(out=s_ln1[:, 2, :], in_=mu_c[:]), deps=[em])
            r.dma("sync", lambda e: e.dma_start(out=s_ln1[:, 3, :], in_=rr_c[:]), deps=[em])
        e_xb = []
        for tb in range(NTILE):
            exn = r.op("vector", lambda e, tb=tb: e.tensor_scalar(
                q_sb[:, 0:D], xtm[:, tb, :], rr_c[:, tb:tb + 1], mr_c[:, tb:tb + 1],
                ALU.mult, ALU.subtract), deps=[em, ev.get("qsb_f")])
            exb = r.dma("sync", lambda e, tb=tb: e.dma_start(
                out=xn_fm[:, :, tb * 128:(tb + 1) * 128], in_=q_sb[:, 0:D], transpose=True), deps=[exn])
            esp = r.dma("sync", lambda e, tb=tb: e.dma_start(
                out=s_xntm[:, tb, :], in_=q_sb[:, 0:D]), deps=[exn])
            ev["qsb_f"] = esp
            e_xb.append(exb)
            e_xb.append(esp)
        ev["xn_ready"] = e_xb[-2]
        fence_ln = r.snap()

        r.op("scalar", lambda e: e.preload_act_table(AF.Exp), sig=False)
        for branch in range(2):
            dproj = d_pl if branch == 0 else d_pg
            dwin_ = d_wlin if branch == 0 else d_wgin
            bin3 = blin3 if branch == 0 else bgin3
            e_wl = None
            for kc in range(8):
                e_wl = r.dma("gpsimd", lambda e, kc=kc, dproj=dproj: e.dma_start(
                    out=w_proj[:, kc, :], in_=dproj[kc * 128:(kc + 1) * 128, :]),
                    deps=fence_ln + [ev.get("pf0")])
                e_wl = r.dma("gpsimd", lambda e, kc=kc, dwin_=dwin_: e.dma_start(
                    out=w_qkv[:, kc * 3:kc * 3 + 3, :],
                    in_=dwin_[kc * 128:(kc + 1) * 128, :]))
            for mc in range(8):
                for half in range(2):
                    b = half
                    emm = None
                    for kc in range(8):
                        emm = r.op("tensor", lambda e, mc=mc, half=half, kc=kc, b=b: e.matmul(
                            bk(ps, b), w_proj[:, kc, mc * 128:(mc + 1) * 128],
                            xn_fm[:, kc, half * 512:(half + 1) * 512],
                            start=(kc == 0), stop=(kc == 7)),
                            deps=[e_wl, ev["xn_ready"], ev.get(f"pj{b}"), ev.get("os0")] if kc == 0 else None,
                            sig=(kc == 7))
                    eev = r.op("vector", lambda e, mc=mc, half=half, b=b: e.tensor_copy(
                        lqs_fm[:, mc, half * 512:(half + 1) * 512], bk(ps, b)),
                        deps=[emm, ev.get("ax_free")])
                    ev[f"pj{b}"] = eev
            ev["lqs"] = eev

            for hg in range(8):
                part_dst = (q_sb, k_sb, vf_sb)
                e_qkv = None
                for part in range(3):
                    cchunk = (part * D + hg * 128) // 128
                    for half in range(2):
                        b = 2 + half
                        emm = None
                        for kc in range(8):
                            emm = r.op("tensor", lambda e, part=part, hg=hg, half=half, kc=kc, b=b: e.matmul(
                                bk(ps, b), w_qkv[:, kc * 3 + part, hg * 128:(hg + 1) * 128],
                                lqs_fm[:, kc, half * 512:(half + 1) * 512],
                                start=(kc == 0), stop=(kc == 7)),
                                deps=[ev["lqs"], ev.get(f"qk{b}"), ev.get(f"pf{part}"), ev.get("st2"), ev.get("st3")] if kc == 0 else None,
                                sig=(kc == 7))
                        e_qkv = r.op("vector", lambda e, part=part, half=half, b=b, cchunk=cchunk, bin3=bin3: e.tensor_scalar(
                            part_dst[part][:, half * 512:(half + 1) * 512], bk(ps, b),
                            bin3[:, cchunk:cchunk + 1], None, ALU.add), deps=[emm])
                        ev[f"qk{b}"] = e_qkv
                e_vtm = r.dma("sync", lambda e: e.dma_start(
                    out=v_tm[:], in_=vf_sb[:], transpose=True), deps=[e_qkv, ev.get("vtm_f")])
                ev["pf2"] = e_vtm

                e_z = None
                for zb in range(2):
                    e_z = r.op("tensor", lambda e, zb=zb: e.matmul(
                        ps[0:128, 2048 + zb * 512:2048 + (zb + 1) * 512],
                        zcol[:], xn_fm[:, 0, 0:512], start=True, stop=False),
                        deps=[ev.get("av_f"), ev["xn_ready"]], sig=(zb == 1))
                for hp in range(2):
                    d0 = hp * 64
                    sum_bks = (0, 1) if hp == 0 else (6, 7)
                    av_sig = None
                    sum_sig = None
                    if branch == 0:
                        exp_ev = [None, None]
                        for wg in range(2):
                            b = 2 + wg
                            emsk = None
                            for wsub in range(4):
                                w = wg * 4 + wsub
                                r.op("tensor", lambda e, w=w, wsub=wsub, b=b, d0=d0: e.matmul(
                                    bk(ps, b)[:, wsub * 128:(wsub + 1) * 128],
                                    k_sb[d0:d0 + 64, w * 128:(w + 1) * 128],
                                    q_sb[d0:d0 + 64, w * 128:(w + 1) * 128],
                                    start=(wsub == 0), stop=False),
                                    deps=[e_qkv, ev.get(f"st{b}")] if wsub == 0 else None, sig=False)
                                emsk = r.op("tensor", lambda e, wsub=wsub, b=b: e.matmul(
                                    bk(ps, b)[:, wsub * 128:(wsub + 1) * 128],
                                    ident[:], tril8[:], start=False, stop=(wsub == 3)),
                                    sig=(wsub == 3))
                            eex = r.op("scalar", lambda e, wg=wg, b=b: e.activation(
                                pT[:, wg * 512:(wg + 1) * 512], bk(ps, b), AF.Exp, scale=cvec[:, 2:3]),
                                deps=[emsk, ev.get("pT_f")])
                            ev[f"st{b}"] = eex
                            exp_ev[wg] = eex
                        for w in range(8):
                            dep0 = [exp_ev[w // 4], e_vtm]
                            if w == 0:
                                dep0 += [ev.get("av_f"), ev.get(f"os{hp}")]
                            av_sig = r.op("tensor", lambda e, w=w, hp=hp, d0=d0: e.matmul(
                                ps[d0:d0 + 64, 2048 + w * 128:2048 + (w + 1) * 128] if hp == 0
                                else ps[64:128, 2048 + w * 128:2048 + (w + 1) * 128],
                                v_tm[:, w, d0:d0 + 64], pT[:, w * 128:(w + 1) * 128],
                                start=False, stop=(w == 3 or w == 7)),
                                deps=dep0 + [e_z], sig=(w == 3 or w == 7))
                            sum_sig = r.op("tensor", lambda e, w=w, hp=hp, d0=d0, sb_=sum_bks: e.matmul(
                                ps[d0:d0 + 64, sb_[w // 4] * 512 + (w % 4) * 128: sb_[w // 4] * 512 + (w % 4) * 128 + 128],
                                onesm[:, d0:d0 + 64], pT[:, w * 128:(w + 1) * 128],
                                start=(w == 0 or w == 4), stop=(w == 3 or w == 7),
                                skip_group_check=True),
                                deps=[ev.get(f"os{hp}"), ev.get("pj0"), ev.get("pj1")] if w == 0 else None,
                                sig=(w == 3 or w == 7))
                        ev["pT_f"] = sum_sig
                    else:
                        for c in range(8):
                            hc = c // 4
                            for half in range(2):
                                b = 2 + half
                                dep0 = [e_qkv, ev.get(f"st{b}")]
                                if half == hc:
                                    r.op("tensor", lambda e, c=c, half=half, b=b, d0=d0: e.matmul(
                                        bk(ps, b), k_sb[d0:d0 + 64, c * 128:(c + 1) * 128],
                                        q_sb[d0:d0 + 64, half * 512:(half + 1) * 512],
                                        start=True, stop=False), deps=dep0, sig=False)
                                    emm = r.op("tensor", lambda e, c=c, b=b: e.matmul(
                                        bk(ps, b)[:, (c % 4) * 128:(c % 4) * 128 + 128],
                                        ident[:], tril8[:], start=False, stop=True), sig=True)
                                else:
                                    emm = r.op("tensor", lambda e, c=c, half=half, b=b, d0=d0: e.matmul(
                                        bk(ps, b), k_sb[d0:d0 + 64, c * 128:(c + 1) * 128],
                                        q_sb[d0:d0 + 64, half * 512:(half + 1) * 512],
                                        start=True, stop=True), deps=dep0, sig=True)
                                bnd = min(max(c * 128 - half * 512, 0), 512)
                                eex = None
                                if bnd > 0:
                                    eex = r.op("scalar", lambda e, half=half, b=b, bnd=bnd: e.activation(
                                        pT[:, half * 512:half * 512 + bnd], bk(ps, b)[:, 0:bnd],
                                        AF.Exp, scale=cvec[:, 2:3], bias=cvec[:, 1:2]), deps=[emm, ev.get("pT_f")])
                                if bnd < 512:
                                    eex = r.op("scalar", lambda e, half=half, b=b, bnd=bnd: e.activation(
                                        pT[:, half * 512 + bnd:(half + 1) * 512], bk(ps, b)[:, bnd:512],
                                        AF.Exp, scale=cvec[:, 2:3]), deps=[emm, ev.get("pT_f")])
                                ev[f"st{b}"] = eex
                                ev[f"expg{half}"] = eex
                            for half in range(2):
                                dep0 = [ev[f"expg{half}"], e_vtm]
                                if c == 0 and half == 0:
                                    dep0 += [ev.get("av_f"), ev.get(f"os{hp}")]
                                av_sig = r.op("tensor", lambda e, c=c, half=half, hp=hp, d0=d0: e.matmul(
                                    ps[d0:d0 + 64, 2048 + half * 512:2048 + (half + 1) * 512] if hp == 0
                                    else ps[64:128, 2048 + half * 512:2048 + (half + 1) * 512],
                                    v_tm[:, c, d0:d0 + 64], pT[:, half * 512:(half + 1) * 512],
                                    start=False, stop=(c == 7)),
                                    deps=dep0 + [e_z], sig=(c == 7))
                                sum_sig = r.op("tensor", lambda e, c=c, half=half, hp=hp, d0=d0, sb_=sum_bks: e.matmul(
                                    ps[d0:d0 + 64, sb_[half] * 512:sb_[half] * 512 + 512],
                                    onesm[:, d0:d0 + 64], pT[:, half * 512:(half + 1) * 512],
                                    start=(c == 0), stop=(c == 7),
                                    skip_group_check=True),
                                    deps=[ev.get(f"os{hp}"), ev.get("pj0"), ev.get("pj1")] if c == 0 else None,
                                    sig=(c == 7))
                            ev["pT_f"] = sum_sig

                    ev[f"os{hp}"] = sum_sig
                    ev[f"erc{hp}"] = sum_sig
                ev["pf0"] = sum_sig
                ev["pf1"] = sum_sig

                eS0 = r.op("vector", lambda e: e.tensor_copy(sums64[0:64, :], ps[0:64, 0:1024]),
                           deps=[ev["os0"], ev.get("sums64_f")])
                eS1 = r.op("vector", lambda e: e.tensor_copy(sums64[64:128, :], ps[64:128, 3072:4096]),
                           deps=[ev["os1"], eS0])
                ev["os0"] = eS0
                ev["os1"] = eS1
                abuf = hg % 2
                eRc = r.op("vector", lambda e: e.reciprocal(acc_f32[:, 0:1024], sums64[:]),
                           deps=[eS1, ev.get("recb_f")])
                eN = r.op("vector", lambda e, abuf=abuf: e.tensor_tensor(
                    atile[:, abuf, :], ps[0:128, 2048:3072], acc_f32[:, 0:1024], ALU.mult),
                    deps=[eRc, ev.get(f"atile_f{abuf}")])
                ev["recb_f"] = eN
                ev["av_f"] = eN
                ev["sums64_f"] = eN
                e_spl = r.dma("sync", lambda e, branch=branch, hg=hg, abuf=abuf: e.dma_start(
                    out=s_att[branch, hg], in_=atile[:, abuf, :]), deps=[eN])
                ev[f"atile_f{abuf}"] = e_spl
                ev["vtm_f"] = e_spl
        fence_att = r.snap()

        e_wo = None
        for kc in range(8):
            e_wo = r.dma("gpsimd", lambda e, kc=kc: e.dma_start(
                out=w_lo[:, kc, :], in_=s_wlo[:, kc, :]), deps=fence_att)
            e_wo = r.dma("gpsimd", lambda e, kc=kc: e.dma_start(
                out=w_go[:, kc, :], in_=s_wgo[:, kc, :]))
        for half in range(2):
            e_as = {}
            for kc in range(16):
                branch, hgk = (0, kc) if kc < 8 else (1, kc - 8)
                e_as[kc] = r.dma("sync", lambda e, branch=branch, hgk=hgk, kc=kc, half=half: e.dma_start(
                    out=astr[:, kc % 2, :], in_=s_att[branch, hgk, :, half * 512:(half + 1) * 512]),
                    deps=fence_att + [ev.get(f"astr_f{kc % 2}")])
                wsel = w_lo if kc < 8 else w_go
                emm8 = None
                for mc in range(8):
                    emm8 = r.op("tensor", lambda e, kc=kc, mc=mc, wsel=wsel, hgk=hgk: e.matmul(
                        bk(ps, mc), wsel[:, hgk, mc * 128:(mc + 1) * 128], astr[:, kc % 2, :],
                        start=(kc == 0), stop=(kc == 15)),
                        deps=[e_as[kc], e_wo] + ([ev.get("op_f")] if kc == 0 else []),
                        sig=(mc == 7))
                ev[f"astr_f{kc % 2}"] = emm8
            e_ops = []
            for mc in range(8):
                eo = r.op("scalar", lambda e, mc=mc, half=half: e.activation(
                    fused_fm[:, mc, half * 512:(half + 1) * 512], bk(ps, mc), AF.Identity,
                    bias=fb8[:, mc:mc + 1]), deps=[emm8, ev["fb8"], ev.get("ax_free")])
                e_ops.append(eo)
            ev["op_f"] = e_ops[-1]
        ev["ax_free"] = e_ops[-1]

        r.op("scalar", lambda e: e.preload_act_table(AF.Sqrt), sig=False)
        e_xs = None
        for dc in range(8):
            e_xs = r.dma("sync", lambda e, dc=dc: e.dma_start(
                out=hid_all[:, :, dc * 128:(dc + 1) * 128],
                in_=fused_fm[:, dc, :], transpose=True),
                deps=[e_ops[-1]] if dc == 0 else None)
        e_st2 = []
        for tb in range(NTILE):
            r.op("scalar", lambda e, tb=tb: e.activation(
                stt_scr[:, 0:1024], hid_all[:, tb, 0:1024], AF.Copy, accum_out=s1c[:, tb:tb + 1]),
                deps=[e_xs] if tb == 0 else None, sig=False)
            e_st2.append(r.op("scalar", lambda e, tb=tb: e.activation(
                stt_scr[:, 1024:2048], hid_all[:, tb, 0:1024], AF.Square, accum_out=s2c[:, tb:tb + 1]),
                drain=(tb == NTILE - 1)))
        eL = r.op("vector", lambda e: e.tensor_scalar(mu_c[:], s1c[:], 1.0 / D, None, ALU.mult), deps=[e_st2[-1]], drain=True)
        eL = r.op("vector", lambda e: e.tensor_copy(mr_c[:], mu_c[:]), deps=[eL], drain=True)
        eL = r.op("vector", lambda e: e.tensor_mul(ln_t1[:], mu_c[:], mr_c[:]), deps=[eL], drain=True)
        eL = r.op("vector", lambda e: e.tensor_scalar(s1c[:], s2c[:], 1.0 / D, None, ALU.mult), deps=[eL], drain=True)
        eL = r.op("vector", lambda e: e.tensor_sub(ln_t1[:], s1c[:], ln_t1[:]), deps=[eL], drain=True)
        eL = r.op("scalar", lambda e: e.activation(s2c[:], ln_t1[:], AF.Sqrt, bias=cvec[:, 0:1]), deps=[eL], drain=True)
        eL = r.op("vector", lambda e: e.reciprocal(rr_c[:], s2c[:]), deps=[eL], drain=True)
        eL = r.op("vector", lambda e: e.tensor_mul(mr_c[:], mu_c[:], rr_c[:]), deps=[eL], drain=True)
        e_ln2 = []
        for tb in range(NTILE):
            eA = r.op("vector", lambda e, tb=tb: e.tensor_scalar(
                k_sb[:, 0:D], hid_all[:, tb, 0:1024], rr_c[:, tb:tb + 1], mr_c[:, tb:tb + 1],
                ALU.mult, ALU.subtract), deps=[eL, ev.get("ksb_f")])
            eA = r.op("vector", lambda e: e.tensor_mul(k_sb[:, 0:D], k_sb[:, 0:D], gam_b[:]), deps=[eA])
            eA = r.op("vector", lambda e: e.tensor_add(k_sb[:, 0:D], k_sb[:, 0:D], bet_b[:]), deps=[eA])
            esx = r.dma("sync", lambda e, tb=tb: e.dma_start(out=s_stm[:, tb, :], in_=k_sb[:, 0:D]), deps=[eA])
            ev["ksb_f"] = esx
            e_ln2.append(esx)
        fence_ln2 = r.snap()

        e_wf = None
        for kc in range(8):
            e_wf = r.dma("gpsimd", lambda e, kc=kc: e.dma_start(
                out=w_win[:, kc, :], in_=d_win[kc * 128:(kc + 1) * 128, :]), deps=fence_ln2)
        e_mtl = r.dma("gpsimd", lambda e: e.dma_start(out=mt_sb[:], in_=s_mt[:]), deps=fence_ln2)
        r.op("scalar", lambda e: e.preload_act_table(AF.Gelu), sig=False)

        for tb in range(NTILE):
            gb = tb % 2
            e_xnr = r.dma("sync", lambda e, tb=tb, gb=gb: e.dma_start(
                out=xn_tmg[:, gb, :], in_=s_xntm[:, tb, :]), deps=[ev.get(f"xnr_f{gb}")] + fence_ln2)
            e_sim = None
            for kc in range(8):
                for nb in range(4):
                    e_sim = r.op("tensor", lambda e, tb=tb, kc=kc, nb=nb: e.matmul(
                        bk(ps, nb), xn_fm[:, kc, tb * 128:(tb + 1) * 128],
                        mt_sb[:, kc, nb * 512:(nb + 1) * 512],
                        start=(kc == 0), stop=(kc == 7)),
                        deps=([e_mtl, ev.get("simf")] + fence_ln2) if (kc == 0 and nb == 0) else None,
                        sig=(kc == 7 and nb == 3))
            e_mx = []
            for ph in range(16):
                nb, col = ph // 4, (ph % 4) * 128
                r.op("vector", lambda e, ph=ph, nb=nb, col=col: e.max(
                    out=mx8[:, ph, :], in_=bk(ps, nb)[:, col:col + 128]),
                    deps=[e_sim] if ph == 0 else None, sig=False)
            for ph in range(16):
                nb, col = ph // 4, (ph % 4) * 128
                e8 = r.op("vector", lambda e, ph=ph, nb=nb, col=col: e.max_index(
                    mi_u[:, ph, :], mx8[:, ph, :], bk(ps, nb)[:, col:col + 128]), deps=None)
                e_mx.append(e8)
            ev["simf"] = e_mx[-1]
            eg = r.op("vector", lambda e: e.tensor_add(
                score8[:], mx8[:, 0:8, 0], mx8[:, 8:16, 0]), deps=[e_mx[-1]], drain=True)
            eg = r.op("vector", lambda e, tb=tb: e.tensor_scalar_max(
                gate_all[:, tb, :], score8[:], 0.0), deps=[eg], drain=True)
            ee = r.op("vector", lambda e, tb=tb: e.tensor_scalar(
                e_all[:, tb, :], mi_u[:, 0:8, 0], 128.0, None, ALU.mult), deps=[e_mx[-1]], drain=True)
            ee = r.op("vector", lambda e, tb=tb: e.tensor_add(
                e_all[:, tb, :], e_all[:, tb, :], mi_u[:, 8:16, 0]), deps=[ee], drain=True)
            ev[f"e_done{tb}"] = ee
            e_wm = None
            for kc in range(8):
                for nb in range(4):
                    e_wm = r.op("tensor", lambda e, tb=tb, kc=kc, nb=nb: e.matmul(
                        bk(ps, 4 + nb), xn_fm[:, kc, tb * 128:(tb + 1) * 128],
                        w_win[:, kc, nb * 512:(nb + 1) * 512],
                        start=(kc == 0), stop=(kc == 7)),
                        deps=[e_wf, ev.get("hidf")] if (kc == 0 and nb == 0) else None,
                        sig=(kc == 7 and nb == 3))
            e_acc = r.op("vector", lambda e: e.tensor_copy(acc_f32[:], ps[0:128, 2048:4096]),
                         deps=[e_wm, ev.get("accf")])
            ev["hidf"] = e_acc
            e_corr = e_acc
            for h in range(8):
                hb = h % 2
                r.dma("gpsimd", lambda e, tb=tb, h=h, hb=hb: e.indirect_dma_start(
                    out=gbuf2[:, hb, 0:1024], out_offset=None, in_=d_ina[:],
                    in_offset=IndirectOffsetOnAxis(ap=e_all[:, tb, h:h + 1], axis=0)),
                    deps=[ee, ev.get(f"gbf{hb}")])
                egt = r.dma("gpsimd", lambda e, tb=tb, h=h, hb=hb: e.indirect_dma_start(
                    out=gbuf2[:, hb, 1024:3072], out_offset=None, in_=d_inb[:],
                    in_offset=IndirectOffsetOnAxis(ap=e_all[:, tb, h:h + 1], axis=0)),
                    deps=None)
                ed = r.op("vector", lambda e, tb=tb, h=h, hb=hb: e.scalar_tensor_tensor(
                    out=stt_scr[:, 0:1024], in0=gbuf2[:, hb, 0:1024],
                    scalar=gate_all[:, tb, h:h + 1], in1=xn_tmg[:, tb % 2, :],
                    op0=ALU.mult, op1=ALU.mult, accum_out=lingc[:, h:h + 1]),
                    deps=[egt, e_xnr, eg], drain=True)
                e_corr = r.op("vector", lambda e, h=h, hb=hb: e.scalar_tensor_tensor(
                    out=acc_f32[:], in0=gbuf2[:, hb, 1024:3072],
                    scalar=lingc[:, h:h + 1], in1=acc_f32[:], op0=ALU.mult, op1=ALU.add),
                    deps=[ed, e_corr])
                ev[f"gbf{hb}"] = e_corr
            e_gel = r.op("scalar", lambda e, tb=tb: e.activation(
                hid_all[:, tb, :], acc_f32[:], AF.Gelu), deps=[e_corr])
            ev["accf"] = e_gel
            ev[f"xnr_f{gb}"] = ed
            ev[f"gelu{tb}"] = e_gel
        if dbg:
            r.dma("sync", lambda e: e.dma_start(out=s_hid[:], in_=hid_all[:]), deps=r.snap())
            r.dma("sync", lambda e: e.dma_start(out=s_edbg[:], in_=e_all[:]))
            r.dma("sync", lambda e: e.dma_start(out=s_gdbg[:], in_=gate_all[:]))
            r.dma("sync", lambda e: e.dma_start(out=s_mxdbg[:], in_=mx8[:]))
        fence_ffna = r.snap()

        e_wf2 = None
        for kc in range(16):
            e_wf2 = r.dma("gpsimd", lambda e, kc=kc: e.dma_start(
                out=w_wout[:, kc, :], in_=d_wout[kc * 128:(kc + 1) * 128, :]), deps=fence_ffna)
        r.op("scalar", lambda e: e.preload_act_table(AF.Silu), sig=False)
        for tb in range(NTILE):
            gb = tb % 2
            e_str = r.dma("sync", lambda e, tb=tb, gb=gb: e.dma_start(
                out=s_tmg[:, gb, :], in_=s_stm[:, tb, :]), deps=[ev.get(f"str_f{gb}")] + fence_ffna)
            e_x2 = r.dma("sync", lambda e, tb=tb, gb=gb, t0=t0: e.dma_start(
                out=x2_t[:, gb, :], in_=dx[t0 + tb * 128:t0 + tb * 128 + 128, :]),
                deps=[ev.get(f"x2_f{gb}")])
            e_hfm = r.dma("sync", lambda e, tb=tb: e.dma_start(
                out=hid_fm[:], in_=hid_all[:, tb, :], transpose=True),
                deps=[ev[f"gelu{tb}"], ev.get("hfm_f")])
            e_wo2 = None
            for kc in range(16):
                for nb in range(2):
                    e_wo2 = r.op("tensor", lambda e, tb=tb, kc=kc, nb=nb: e.matmul(
                        bk(ps, nb), hid_fm[:, kc, :], w_wout[:, kc, nb * 512:(nb + 1) * 512],
                        start=(kc == 0), stop=(kc == 15)),
                        deps=[e_wf2, e_hfm, ev.get("outf")] if (kc == 0 and nb == 0) else None,
                        sig=(kc == 15 and nb == 1))
            ev["hfm_f"] = e_wo2
            e_oacc = r.op("vector", lambda e: e.tensor_copy(acc_f32[:, 0:1024], ps[0:128, 0:1024]),
                          deps=[e_wo2, ev.get("accf2")])
            ev["outf"] = e_oacc
            e_corr = e_oacc
            for h in range(8):
                hb = h % 2
                r.dma("gpsimd", lambda e, tb=tb, h=h, hb=hb: e.indirect_dma_start(
                    out=gbuf2[:, hb, 0:2048], out_offset=None, in_=d_outa[:],
                    in_offset=IndirectOffsetOnAxis(ap=e_all[:, tb, h:h + 1], axis=0)),
                    deps=[ev[f"e_done{tb}"], ev.get(f"gbf{hb}")])
                egt = r.dma("gpsimd", lambda e, tb=tb, h=h, hb=hb: e.indirect_dma_start(
                    out=gbuf2[:, hb, 2048:3072], out_offset=None, in_=d_outb[:],
                    in_offset=IndirectOffsetOnAxis(ap=e_all[:, tb, h:h + 1], axis=0)),
                    deps=None)
                ed = r.op("vector", lambda e, tb=tb, h=h, hb=hb: e.scalar_tensor_tensor(
                    out=stt_scr[:, 0:2048], in0=gbuf2[:, hb, 0:2048],
                    scalar=gate_all[:, tb, h:h + 1], in1=hid_all[:, tb, :],
                    op0=ALU.mult, op1=ALU.mult, accum_out=lingc[:, h:h + 1]),
                    deps=[egt, ev[f"gelu{tb}"]], drain=True)
                e_corr = r.op("vector", lambda e, h=h, hb=hb: e.scalar_tensor_tensor(
                    out=acc_f32[:, 0:1024], in0=gbuf2[:, hb, 2048:3072],
                    scalar=lingc[:, h:h + 1], in1=acc_f32[:, 0:1024], op0=ALU.mult, op1=ALU.add),
                    deps=[ed, e_corr])
                ev[f"gbf{hb}"] = e_corr
            e_sil = r.op("scalar", lambda e: e.activation(g_t[:], acc_f32[:, 0:1024], AF.Silu),
                         deps=[e_corr, ev.get("gt_f")])
            pass
            e_p = r.op("vector", lambda e, gb=gb: e.tensor_mul(
                acc_f32[:, 1024:2048], g_t[:], s_tmg[:, gb, :]), deps=[e_sil, e_str])
            ev["gt_f"] = e_p
            ev[f"str_f{gb}"] = e_p
            ev["accf2"] = e_p
            e_o = r.op("vector", lambda e, gb=gb: e.tensor_add(
                x2_t[:, gb, :], x2_t[:, gb, :], acc_f32[:, 1024:2048]), deps=[e_p, e_x2, ev.get(f"out_f{gb}")])
            e_wr = r.dma("sync", lambda e, tb=tb, gb=gb, t0=t0: e.dma_start(
                out=d_out[t0 + tb * 128:t0 + tb * 128 + 128, :], in_=x2_t[:, gb, :]), deps=[e_o])
            ev[f"x2_f{gb}"] = e_wr
            ev[f"out_f{gb}"] = e_wr
            e_out_final = e_wr
        ev["arena_w_free"] = e_wo2

    r.wait("sync", [e_out_final])
    r.emit()
    return nc



def _prep_inputs(inp):
    x = np.ascontiguousarray(inp["L0_attn_x"])
    common = {
        "w_pl": np.ascontiguousarray(inp["L0_attn_proj_local"]),
        "w_pg": np.ascontiguousarray(inp["L0_attn_proj_global"]),
        "w_l_inT": np.ascontiguousarray(inp["L0_attn_Wl_in"].T),
        "w_g_inT": np.ascontiguousarray(inp["L0_attn_Wg_in"].T),
        "b_l_in3": np.ascontiguousarray(inp["L0_attn_bl_in"].reshape(24, 128).T),
        "b_g_in3": np.ascontiguousarray(inp["L0_attn_bg_in"].reshape(24, 128).T),
        "w_l_outT": np.ascontiguousarray(inp["L0_attn_Wl_out"].T),
        "w_g_outT": np.ascontiguousarray(inp["L0_attn_Wg_out"].T),
        "b_l_out8": np.ascontiguousarray(inp["L0_attn_bl_out"].reshape(8, 128).T),
        "b_g_out8": np.ascontiguousarray(inp["L0_attn_bg_out"].reshape(8, 128).T),
        "fusion_w": np.ascontiguousarray(inp["L0_attn_fusion_w"].reshape(1, 2)),
        "gamma_b": np.ascontiguousarray(np.broadcast_to(inp["L0_attn_ln_gamma"], (128, D))),
        "beta_b": np.ascontiguousarray(np.broadcast_to(inp["L0_attn_ln_beta"], (128, D))),
        "wq": np.ascontiguousarray(inp["L0_ffn_Wq"]),
        "keysT": np.ascontiguousarray(inp["L0_ffn_keys"].transpose(2, 0, 3, 1).reshape(16, 512, 128)),
        "w_inT": np.ascontiguousarray(inp["L0_ffn_W_in"].T),
        "w_outT": np.ascontiguousarray(inp["L0_ffn_W_out"].T),
        "tab_ina": np.ascontiguousarray(inp["L0_ffn_in_a"]),
        "tab_inb": np.ascontiguousarray(inp["L0_ffn_in_b"]),
        "tab_outa": np.ascontiguousarray(inp["L0_ffn_out_a"]),
        "tab_outb": np.ascontiguousarray(inp["L0_ffn_out_b"]),
        "ident_bf": np.eye(128, dtype=BF),
        "tril8_bf": (8.0 * np.tril(np.ones((128, 128), np.float32), k=-1)).astype(BF),
        "onesrow": np.ones((1, 128), np.float32),
        "onescol": np.ones((128, 1), BF),
        "onesmat": np.ones((128, 128), BF),
        "onesrow_bf": np.ones((1, 128), BF),
        "zerocol": np.zeros((128, 128), BF),
        "constv": np.ascontiguousarray(np.broadcast_to(
            np.array([1e-5, 1.0, 0.125, 0.0], np.float32), (128, 4))),
    }
    in_maps = []
    for c in range(NCORES):
        b, q0 = c // 4, (c % 4) * T
        m = dict(common)
        m["x_rows"] = np.ascontiguousarray(x[b, q0:q0 + T, :])
        in_maps.append(m)
    return in_maps


def kernel(**inputs):
    in_maps = _prep_inputs(inputs)
    nc = build_program()
    res = run_bass_kernel_spmd(nc, in_maps, core_ids=list(range(NCORES)))
    out = np.zeros((2, 4 * T, D), np.float32)
    for c in range(NCORES):
        b, q0 = c // 4, (c % 4) * T
        out[b, q0:q0 + T, :] = res.results[c]["out_rows"]
    return out


# version 09 (final)
# speedup vs baseline: 1.2051x; 1.2051x over previous
import numpy as np
import ml_dtypes
import concourse.bass as bass
import concourse.mybir as mybir
from concourse.bass import IndirectOffsetOnAxis
from concourse.bass_utils import run_bass_kernel_spmd

F32 = mybir.dt.float32
BF16 = mybir.dt.bfloat16
U32 = mybir.dt.uint32
AF = mybir.ActivationFunctionType
ALU = mybir.AluOpType
AX = mybir.AxisListType
BF = ml_dtypes.bfloat16

NCORES = 8
T = 4096
D = 1024
ST = 1024
NST = T // ST
NTILE = ST // 128
LN_EPS = 1e-5

COMPUTE_SEM = {"tensor": "cT", "vector": "cV", "scalar": "cS", "gpsimd": "cG"}
DMA_SEM = {"sync": "dS", "gpsimd": "dG", "scalar": "dC"}


class Rec:
    def __init__(self, nc):
        self.nc = nc
        self.stream = {e: [] for e in ("tensor", "vector", "scalar", "gpsimd", "sync")}
        self.cnt = {s: 0 for s in ("cT", "cV", "cS", "cG", "dS", "dG", "dC")}
        self.lastwait = {e: {} for e in self.stream}
        self.gctr = 0

    def wait(self, eng, evs):
        if evs is None:
            return
        if isinstance(evs, tuple) and len(evs) == 2 and isinstance(evs[0], str):
            evs = [evs]
        own = COMPUTE_SEM.get(eng)
        for ev in evs:
            if ev is None:
                continue
            sem, val = ev
            if sem == own:
                continue
            if self.lastwait[eng].get(sem, -1) >= val:
                continue
            self.lastwait[eng][sem] = val
            self.stream[eng].append(("w", sem, val))

    def op(self, eng, fn, deps=None, sig=True, drain=False):
        self.wait(eng, deps)
        if drain:
            self.stream[eng].append(("o", fn, None, 0))
            if sig:
                sem = COMPUTE_SEM[eng]
                self.cnt[sem] += 1
                self.stream[eng].append(("o", lambda e: e.drain(), sem, 1))
                return (sem, self.cnt[sem])
            self.stream[eng].append(("o", lambda e: e.drain(), None, 0))
            return None
        if sig:
            sem = COMPUTE_SEM[eng]
            self.cnt[sem] += 1
            self.stream[eng].append(("o", fn, sem, 1))
            return (sem, self.cnt[sem])
        self.stream[eng].append(("o", fn, None, 0))
        return None

    def dma(self, eng, fn, deps=None):
        self.wait(eng, deps)
        sem = DMA_SEM[eng]
        self.cnt[sem] += 16
        self.stream[eng].append(("o", fn, sem, 16))
        return (sem, self.cnt[sem])

    def snap(self):
        return [(s, v) for s, v in self.cnt.items() if v > 0]

    def emit(self):
        nc = self.nc
        sems = {name: nc.alloc_semaphore(name) for name in self.cnt}
        with nc.Block() as block:
            def runner(name):
                def run(eng):
                    for item in self.stream[name]:
                        if item[0] == "w":
                            eng.wait_ge(sems[item[1]], item[2])
                        else:
                            inst = item[1](eng)
                            if item[2] is not None:
                                inst.then_inc(sems[item[2]], item[3])
                return run
            block.tensor(runner("tensor"))
            block.vector(runner("vector"))
            block.scalar(runner("scalar"))
            block.gpsimd(runner("gpsimd"))
            block.sync(runner("sync"))


def bk(ps, b):
    return ps[0:128, b * 512:(b + 1) * 512]


def build_program(dbg=False):
    nc = bass.Bass("TRN2", target_bir_lowering=False, detect_race_conditions=False)
    r = Rec(nc)

    dx = nc.dram_tensor("x_rows", [T, D], F32, kind="ExternalInput")
    d_pl = nc.dram_tensor("w_pl", [D, D], F32, kind="ExternalInput")
    d_pg = nc.dram_tensor("w_pg", [D, D], F32, kind="ExternalInput")
    d_wlin = nc.dram_tensor("w_l_inT", [D, 3 * D], F32, kind="ExternalInput")
    d_wgin = nc.dram_tensor("w_g_inT", [D, 3 * D], F32, kind="ExternalInput")
    d_blin = nc.dram_tensor("b_l_in3", [128, 24], F32, kind="ExternalInput")
    d_bgin = nc.dram_tensor("b_g_in3", [128, 24], F32, kind="ExternalInput")
    d_wlout = nc.dram_tensor("w_l_outT", [D, D], F32, kind="ExternalInput")
    d_wgout = nc.dram_tensor("w_g_outT", [D, D], F32, kind="ExternalInput")
    d_blout = nc.dram_tensor("b_l_out8", [128, 8], F32, kind="ExternalInput")
    d_bgout = nc.dram_tensor("b_g_out8", [128, 8], F32, kind="ExternalInput")
    d_fw = nc.dram_tensor("fusion_w", [1, 2], F32, kind="ExternalInput")
    d_gam = nc.dram_tensor("gamma_b", [128, D], F32, kind="ExternalInput")
    d_bet = nc.dram_tensor("beta_b", [128, D], F32, kind="ExternalInput")
    d_wq = nc.dram_tensor("wq", [8 * D, D], F32, kind="ExternalInput")
    d_keysT = nc.dram_tensor("keysT", [16, 512, 128], F32, kind="ExternalInput")
    d_win = nc.dram_tensor("w_inT", [D, 2 * D], F32, kind="ExternalInput")
    d_wout = nc.dram_tensor("w_outT", [2 * D, D], F32, kind="ExternalInput")
    d_ina = nc.dram_tensor("tab_ina", [16384, D], F32, kind="ExternalInput")
    d_inb = nc.dram_tensor("tab_inb", [16384, 2 * D], F32, kind="ExternalInput")
    d_outa = nc.dram_tensor("tab_outa", [16384, 2 * D], F32, kind="ExternalInput")
    d_outb = nc.dram_tensor("tab_outb", [16384, D], F32, kind="ExternalInput")
    d_ident = nc.dram_tensor("ident_bf", [128, 128], BF16, kind="ExternalInput")
    d_tril = nc.dram_tensor("tril8_bf", [128, 128], BF16, kind="ExternalInput")
    d_ones = nc.dram_tensor("onesrow", [1, 128], F32, kind="ExternalInput")
    d_onesc = nc.dram_tensor("onescol", [128, 1], BF16, kind="ExternalInput")
    d_onesm = nc.dram_tensor("onesmat", [128, 128], BF16, kind="ExternalInput")
    d_onesr_bf = nc.dram_tensor("onesrow_bf", [1, 128], BF16, kind="ExternalInput")
    d_zc = nc.dram_tensor("zerocol", [128, 128], BF16, kind="ExternalInput")
    d_cv = nc.dram_tensor("constv", [128, 4], F32, kind="ExternalInput")

    d_out = nc.dram_tensor("out_rows", [T, D], F32, kind="ExternalOutput")

    KD = "ExternalOutput" if dbg else "Internal"
    s_mt = nc.dram_tensor("mt_scr", [128, 8, 2 * D], BF16, kind=KD)
    s_wlo = nc.dram_tensor("wlo_scr", [128, 8, D], BF16, kind=KD)
    s_wgo = nc.dram_tensor("wgo_scr", [128, 8, D], BF16, kind=KD)
    s_xntm = nc.dram_tensor("xntm_scr", [128, NTILE, D], BF16, kind=KD)
    s_stm = nc.dram_tensor("stm_scr", [128, NTILE, D], BF16, kind=KD)
    s_att = nc.dram_tensor("att_scr", [2, 8, 128, ST], BF16, kind=KD)
    if dbg:
        s_hid = nc.dram_tensor("hid_dbg", [128, NTILE, 2 * D], BF16, kind=KD)
        s_edbg = nc.dram_tensor("e_dbg", [128, NTILE, 8], U32, kind=KD)
        s_ln1 = nc.dram_tensor("ln1_dbg", [128, 5, NTILE], F32, kind=KD)
        s_gdbg = nc.dram_tensor("g_dbg", [128, NTILE, 8], F32, kind=KD)
        s_mxdbg = nc.dram_tensor("mx_dbg", [128, 16, 8], F32, kind=KD)

    A = nc.alloc_sbuf_tensor
    ident = A("ident", [128, 128], BF16)
    tril8 = A("tril8", [128, 128], BF16)
    onesf = A("onesf", [1, 128], F32)
    onesc = A("onesc", [128, 1], BF16)
    onesm = A("onesm", [128, 128], BF16)
    onesr_bf = A("onesr_bf", [1, 128], BF16)
    zcol = A("zcol", [128, 128], BF16)
    cvec = A("cvec", [128, 4], F32)
    gam_b = A("gam_b", [128, D], F32)
    bet_b = A("bet_b", [128, D], F32)
    blin3 = A("blin3", [128, 24], F32)
    bgin3 = A("bgin3", [128, 24], F32)
    fw_sb = A("fw_sb", [1, 2], F32)
    fwtmp = A("fwtmp", [1, 4], F32)
    w01 = A("w01", [1, 2], F32)
    w01b = A("w01b", [128, 2], F32)
    fb8 = A("fb8", [128, 8], F32)
    blo8 = A("blo8", [128, 8], F32)
    bgo8 = A("bgo8", [128, 8], F32)
    s1c = A("s1c", [128, NTILE], F32)
    s2c = A("s2c", [128, NTILE], F32)
    mu_c = A("mu_c", [128, NTILE], F32)
    rr_c = A("rr_c", [128, NTILE], F32)
    mr_c = A("mr_c", [128, NTILE], F32)
    ln_t1 = A("ln_t1", [128, NTILE], F32)

    xn_fm = A("xn_fm", [128, 8, ST], BF16)
    arena_w = A("arena_w", [128, 32768], BF16)
    w_proj = arena_w.reshape([128, 32, D])[:, 0:8, :]
    w_qkv = arena_w.reshape([128, 32, D])[:, 8:32, :]
    w_lo = arena_w.reshape([128, 32, D])[:, 0:8, :]
    w_go = arena_w.reshape([128, 32, D])[:, 8:16, :]
    w_win = arena_w.reshape([128, 16, 2 * D])[:, 0:8, :]
    mt_sb = arena_w.reshape([128, 16, 2 * D])[:, 8:16, :]
    w_wout = arena_w.reshape([128, 32, D])[:, 16:32, :]
    xtm = arena_w.bitcast(F32).reshape([128, 16, D])[:, 0:8, :]
    wq_stage = arena_w.reshape([128, 32, D])[:, 0:4, :]
    keys_stage = arena_w.reshape([128, 32 * D])[:, 4 * D:4 * D + 512]
    mt_stage = arena_w.reshape([128, 32, D])[:, 6:14, 0:128]
    wsc_stage = arena_w.bitcast(F32).reshape([128, 16, D])[:, 8:9, :]
    wsc_out = arena_w.reshape([128, 32, D])[:, 20:21, :]

    arena_x = A("arena_x", [128, 24576 // 2], BF16)
    lqs_fm = arena_x.reshape([128, 12, D])[:, 0:8, :]
    fused_fm = arena_x.reshape([128, 12, D])[:, 0:8, :]
    gbuf2 = arena_x.reshape([128, 4, 3072])

    hid_all = A("hid_all", [128, NTILE, 2 * D], BF16)
    q_sb = A("q_sb", [128, ST], BF16)
    k_sb = A("k_sb", [128, ST], BF16)
    vf_sb = A("vf_sb", [128, ST], BF16)
    v_tm = A("v_tm", [128, 8, 128], BF16)
    pT = A("pT", [128, ST], BF16)
    atile = A("atile", [128, 2, ST], BF16)
    sums64 = arena_x.bitcast(F32).reshape([128, 6144])[:, 4096:5120]
    astr = A("astr", [128, 2, 512], BF16)
    mx8 = A("mx8", [128, 16, 8], F32)
    mi_u = A("mi_u", [128, 16, 8], U32)
    score8 = A("score8", [128, 8], F32)
    gate_all = A("gate_all", [128, NTILE, 8], F32)
    e_all = A("e_all", [128, NTILE, 8], U32)
    stt_scr = A("stt_scr", [128, 3072], BF16)
    acc_f32 = A("acc_f32", [128, 2 * D], F32)
    xn_tmg = A("xn_tmg", [128, 2, D], BF16)
    s_tmg = A("s_tmg", [128, 2, D], BF16)
    hid_fm = A("hid_fm", [128, 2, 16, 128], BF16)
    g_t = A("g_t", [128, 2, D], BF16)
    x2_t = A("x2_t", [128, 2, D], F32)
    lingc = A("lingc", [128, 8], F32)

    ps = nc.alloc_psum_tensor("ps", [128, 4096], F32)

    rem = nc.sbuf_bytes_remaining
    assert rem >= 0, f"SBUF over budget by {-rem} bytes"

    ev = {}

    for dst, src in ((ident, d_ident), (tril8, d_tril), (onesf, d_ones),
                     (onesc, d_onesc), (onesm, d_onesm), (onesr_bf, d_onesr_bf), (zcol, d_zc), (cvec, d_cv),
                     (gam_b, d_gam), (bet_b, d_bet), (blin3, d_blin),
                     (bgin3, d_bgin), (fw_sb, d_fw), (blo8, d_blout),
                     (bgo8, d_bgout)):
        ec = r.dma("sync", lambda e, dst=dst, src=src: e.dma_start(out=dst[:], in_=src[:]))

    r.op("scalar", lambda e: e.preload_act_table(AF.Exp), sig=False)
    e1 = r.op("vector", lambda e: e.reduce_max(fwtmp[:, 0:1], fw_sb[:], axis=AX.X), deps=[ec], drain=True)
    e1 = r.op("vector", lambda e: e.tensor_scalar(fwtmp[:, 1:2], fwtmp[:, 0:1], -1.0, None, ALU.mult), deps=[e1], drain=True)
    e1 = r.op("scalar", lambda e: e.activation(w01[:], fw_sb[:], AF.Exp, bias=fwtmp[:, 1:2]), deps=[e1], drain=True)
    e1 = r.op("vector", lambda e: e.reduce_sum(fwtmp[:, 0:1], w01[:], axis=AX.X), deps=[e1], drain=True)
    e1 = r.op("vector", lambda e: e.reciprocal(fwtmp[:, 1:2], fwtmp[:, 0:1]), deps=[e1], drain=True)
    e1 = r.op("vector", lambda e: e.tensor_scalar(fwtmp[:, 2:4], w01[:], fwtmp[:, 1:2], None, ALU.mult), deps=[e1], drain=True)
    e1t = r.op("tensor", lambda e: e.matmul(ps[0:128, 2048:2050], onesf[:], fwtmp[:, 2:4], start=True, stop=True), deps=[e1])
    e_w01b = r.op("vector", lambda e: e.tensor_copy(w01b[:], ps[0:128, 2048:2050]), deps=[e1t], drain=True)
    ev["av_f"] = e_w01b
    e2 = r.op("vector", lambda e: e.tensor_scalar(fb8[:], blo8[:], w01b[:, 0:1], None, ALU.mult), deps=[e_w01b], drain=True)
    ev["fb8"] = r.op("vector", lambda e: e.scalar_tensor_tensor(
        out=fb8[:], in0=bgo8[:], scalar=w01b[:, 1:2], in1=fb8[:], op0=ALU.mult, op1=ALU.add), deps=[e2], drain=True)

    e_prev = None
    for (dsrc, dscr, wcol) in ((d_wlout, s_wlo, 0), (d_wgout, s_wgo, 1)):
        for kc in range(8):
            eld = r.dma("sync", lambda e, dsrc=dsrc, kc=kc: e.dma_start(
                out=wsc_stage[:, 0, :], in_=dsrc[kc * 128:(kc + 1) * 128, :]), deps=[e_prev])
            esc = r.op("scalar", lambda e, wcol=wcol: e.activation(
                wsc_out[:, 0, :], wsc_stage[:, 0, :], AF.Copy, scale=w01b[:, wcol:wcol + 1]),
                deps=[eld, e_w01b])
            e_prev = r.dma("sync", lambda e, dscr=dscr, kc=kc: e.dma_start(
                out=dscr[:, kc, :], in_=wsc_out[:, 0, :]), deps=[esc])
    ev["wscaled"] = e_prev

    e_sp = ev["wscaled"]
    for ph in range(16):
        ewq = r.dma("gpsimd", lambda e, ph=ph: e.dma_start(
            out=wq_stage[:], in_=d_wq[ph * 512:(ph + 1) * 512, :].rearrange("(k p) d -> p k d", p=128)),
            deps=[e_sp])
        eky = r.dma("gpsimd", lambda e, ph=ph: e.dma_start(
            out=keys_stage.rearrange("p (k d) -> p k d", k=4), in_=d_keysT[ph].rearrange("(k p) d -> p k d", p=128)), deps=[e_sp])
        emms = []
        for mc in range(8):
            b = mc % 4
            col = (mc // 4) * 128
            emm = None
            for kc in range(4):
                emm = r.op("tensor", lambda e, mc=mc, kc=kc, b=b, col=col: e.matmul(
                    bk(ps, b)[:, col:col + 128],
                    wq_stage[:, kc, mc * 128:(mc + 1) * 128],
                    keys_stage[:, kc * 128:(kc + 1) * 128],
                    start=(kc == 0), stop=(kc == 3)),
                    deps=[ewq, eky, ev.get(f"mtb{b}{mc // 4}")] if kc == 0 else None,
                    sig=(kc == 3))
            emms.append(emm)
        eevs = []
        for mc in range(8):
            b = mc % 4
            col = (mc // 4) * 128
            eev = r.op("vector", lambda e, mc=mc, b=b, col=col: e.tensor_copy(
                mt_stage[:, mc, :], bk(ps, b)[:, col:col + 128]), deps=[emms[mc], e_sp])
            ev[f"mtb{b}{mc // 4}"] = eev
            eevs.append(eev)
        e_sp = r.dma("sync", lambda e, ph=ph: e.dma_start(
            out=s_mt[:, :, ph * 128:(ph + 1) * 128], in_=mt_stage[:]), deps=eevs)
    ev["mt_done"] = e_sp

    e_out_final = None

    for st in range(NST):
        t0 = st * ST
        fence0 = r.snap()

        e_x = []
        for tb in range(NTILE):
            e_x.append(r.dma("sync", lambda e, tb=tb, t0=t0: e.dma_start(
                out=xtm[:, tb, :], in_=dx[t0 + tb * 128:t0 + tb * 128 + 128, :]), deps=fence0))
        e_st = []
        for tb in range(NTILE):
            r.op("scalar", lambda e, tb=tb: e.activation(
                stt_scr[:, 0:1024], xtm[:, tb, :], AF.Copy, accum_out=s1c[:, tb:tb + 1]),
                deps=[e_x[tb]] + fence0, sig=False)
            e_st.append(r.op("scalar", lambda e, tb=tb: e.activation(
                stt_scr[:, 1024:2048], xtm[:, tb, :], AF.Square, accum_out=s2c[:, tb:tb + 1]),
                drain=(tb == NTILE - 1)))
        em = r.op("vector", lambda e: e.tensor_scalar(mu_c[:], s1c[:], 1.0 / D, None, ALU.mult), deps=[e_st[-1]] + fence0, drain=True)
        em = r.op("vector", lambda e: e.tensor_copy(mr_c[:], mu_c[:]), deps=[em], drain=True)
        em = r.op("vector", lambda e: e.tensor_mul(ln_t1[:], mu_c[:], mr_c[:]), deps=[em], drain=True)
        em = r.op("vector", lambda e: e.tensor_scalar(s1c[:], s2c[:], 1.0 / D, None, ALU.mult), deps=[em], drain=True)
        em = r.op("vector", lambda e: e.tensor_sub(ln_t1[:], s1c[:], ln_t1[:]), deps=[em], drain=True)
        r.op("scalar", lambda e: e.preload_act_table(AF.Sqrt), sig=False)
        em = r.op("scalar", lambda e: e.activation(s2c[:], ln_t1[:], AF.Sqrt, bias=cvec[:, 0:1]), deps=[em], drain=True)
        em = r.op("vector", lambda e: e.reciprocal(rr_c[:], s2c[:]), deps=[em], drain=True)
        em = r.op("vector", lambda e: e.tensor_mul(mr_c[:], mu_c[:], rr_c[:]), deps=[em], drain=True)
        if dbg:
            r.dma("sync", lambda e: e.dma_start(out=s_ln1[:, 0, :], in_=s1c[:]), deps=[em])
            r.dma("sync", lambda e: e.dma_start(out=s_ln1[:, 1, :], in_=ln_t1[:]), deps=[em])
            r.dma("sync", lambda e: e.dma_start(out=s_ln1[:, 4, :], in_=s2c[:]), deps=[em])
            r.dma("sync", lambda e: e.dma_start(out=s_ln1[:, 2, :], in_=mu_c[:]), deps=[em])
            r.dma("sync", lambda e: e.dma_start(out=s_ln1[:, 3, :], in_=rr_c[:]), deps=[em])
        e_xb = []
        for tb in range(NTILE):
            exn = r.op("vector", lambda e, tb=tb: e.tensor_scalar(
                q_sb[:, 0:D], xtm[:, tb, :], rr_c[:, tb:tb + 1], mr_c[:, tb:tb + 1],
                ALU.mult, ALU.subtract), deps=[em, ev.get("qsb_f")])
            exb = r.dma("sync", lambda e, tb=tb: e.dma_start(
                out=xn_fm[:, :, tb * 128:(tb + 1) * 128], in_=q_sb[:, 0:D], transpose=True), deps=[exn])
            esp = r.dma("sync", lambda e, tb=tb: e.dma_start(
                out=s_xntm[:, tb, :], in_=q_sb[:, 0:D]), deps=[exn])
            ev["qsb_f"] = esp
            e_xb.append(exb)
            e_xb.append(esp)
        ev["xn_ready"] = e_xb[-2]
        fence_ln = r.snap()

        r.op("scalar", lambda e: e.preload_act_table(AF.Exp), sig=False)
        for branch in range(2):
            dproj = d_pl if branch == 0 else d_pg
            dwin_ = d_wlin if branch == 0 else d_wgin
            bin3 = blin3 if branch == 0 else bgin3
            e_wl = None
            for kc in range(8):
                e_wl = r.dma("gpsimd", lambda e, kc=kc, dproj=dproj: e.dma_start(
                    out=w_proj[:, kc, :], in_=dproj[kc * 128:(kc + 1) * 128, :]),
                    deps=fence_ln + [ev.get("pf0")])
                e_wl = r.dma("gpsimd", lambda e, kc=kc, dwin_=dwin_: e.dma_start(
                    out=w_qkv[:, kc * 3:kc * 3 + 3, :],
                    in_=dwin_[kc * 128:(kc + 1) * 128, :]))
            for mc in range(8):
                for half in range(2):
                    b = half
                    emm = None
                    for kc in range(8):
                        emm = r.op("tensor", lambda e, mc=mc, half=half, kc=kc, b=b: e.matmul(
                            bk(ps, b), w_proj[:, kc, mc * 128:(mc + 1) * 128],
                            xn_fm[:, kc, half * 512:(half + 1) * 512],
                            start=(kc == 0), stop=(kc == 7)),
                            deps=[e_wl, ev["xn_ready"], ev.get(f"pj{b}"), ev.get("os0")] if kc == 0 else None,
                            sig=(kc == 7))
                    eev = r.op("scalar", lambda e, mc=mc, half=half, b=b: e.activation(
                        lqs_fm[:, mc, half * 512:(half + 1) * 512], bk(ps, b), AF.Copy),
                        deps=[emm, ev.get("ax_free")])
                    ev[f"pj{b}"] = eev
            ev["lqs"] = eev

            for hg in range(8):
                part_dst = (q_sb, k_sb, vf_sb)
                e_qkv = None
                for part in range(3):
                    cchunk = (part * D + hg * 128) // 128
                    for half in range(2):
                        b = 2 + half
                        emm = None
                        for kc in range(8):
                            emm = r.op("tensor", lambda e, part=part, hg=hg, half=half, kc=kc, b=b: e.matmul(
                                bk(ps, b), w_qkv[:, kc * 3 + part, hg * 128:(hg + 1) * 128],
                                lqs_fm[:, kc, half * 512:(half + 1) * 512],
                                start=(kc == 0), stop=(kc == 7)),
                                deps=[ev["lqs"], ev.get(f"qk{b}"), ev.get(f"pf{part}"), ev.get("st2"), ev.get("st3")] if kc == 0 else None,
                                sig=(kc == 7))
                        e_qkv = r.op("scalar", lambda e, part=part, half=half, b=b, cchunk=cchunk, bin3=bin3: e.activation(
                            part_dst[part][:, half * 512:(half + 1) * 512], bk(ps, b), AF.Identity,
                            bias=bin3[:, cchunk:cchunk + 1]), deps=[emm])
                        ev[f"qk{b}"] = e_qkv
                e_vtm = r.dma("sync", lambda e: e.dma_start(
                    out=v_tm[:], in_=vf_sb[:], transpose=True), deps=[e_qkv, ev.get("vtm_f")])
                ev["pf2"] = e_vtm

                e_z = None
                for zb in range(2):
                    e_z = r.op("tensor", lambda e, zb=zb: e.matmul(
                        ps[0:128, 2048 + zb * 512:2048 + (zb + 1) * 512],
                        zcol[:], xn_fm[:, 0, 0:512], start=True, stop=False),
                        deps=[ev.get("av_f"), ev["xn_ready"]], sig=(zb == 1))
                for hp in range(2):
                    d0 = hp * 64
                    sum_bks = (0, 1) if hp == 0 else (6, 7)
                    av_sig = None
                    sum_sig = None
                    if branch == 0:
                        exp_ev = [None, None]
                        for wg in range(2):
                            b = 2 + wg
                            emsk = None
                            for wsub in range(4):
                                w = wg * 4 + wsub
                                r.op("tensor", lambda e, w=w, wsub=wsub, b=b, d0=d0: e.matmul(
                                    bk(ps, b)[:, wsub * 128:(wsub + 1) * 128],
                                    k_sb[d0:d0 + 64, w * 128:(w + 1) * 128],
                                    q_sb[d0:d0 + 64, w * 128:(w + 1) * 128],
                                    start=(wsub == 0), stop=False),
                                    deps=[e_qkv, ev.get(f"st{b}")] if wsub == 0 else None, sig=False)
                                emsk = r.op("tensor", lambda e, wsub=wsub, b=b: e.matmul(
                                    bk(ps, b)[:, wsub * 128:(wsub + 1) * 128],
                                    ident[:], tril8[:], start=False, stop=(wsub == 3)),
                                    sig=(wsub == 3))
                            eex = r.op("scalar", lambda e, wg=wg, b=b: e.activation(
                                pT[:, wg * 512:(wg + 1) * 512], bk(ps, b), AF.Exp, scale=cvec[:, 2:3]),
                                deps=[emsk, ev.get("pT_f")])
                            ev[f"st{b}"] = eex
                            exp_ev[wg] = eex
                        for w in range(8):
                            dep0 = [exp_ev[w // 4], e_vtm]
                            if w == 0:
                                dep0 += [ev.get("av_f"), ev.get(f"os{hp}")]
                            av_sig = r.op("tensor", lambda e, w=w, hp=hp, d0=d0: e.matmul(
                                ps[d0:d0 + 64, 2048 + w * 128:2048 + (w + 1) * 128] if hp == 0
                                else ps[64:128, 2048 + w * 128:2048 + (w + 1) * 128],
                                v_tm[:, w, d0:d0 + 64], pT[:, w * 128:(w + 1) * 128],
                                start=False, stop=(w == 3 or w == 7)),
                                deps=dep0 + [e_z], sig=(w == 3 or w == 7))
                            sum_sig = r.op("tensor", lambda e, w=w, hp=hp, d0=d0, sb_=sum_bks: e.matmul(
                                ps[d0:d0 + 64, sb_[w // 4] * 512 + (w % 4) * 128: sb_[w // 4] * 512 + (w % 4) * 128 + 128],
                                onesm[:, d0:d0 + 64], pT[:, w * 128:(w + 1) * 128],
                                start=(w == 0 or w == 4), stop=(w == 3 or w == 7),
                                skip_group_check=True),
                                deps=[ev.get(f"os{hp}"), ev.get("pj0"), ev.get("pj1")] if w == 0 else None,
                                sig=(w == 3 or w == 7))
                        ev["pT_f"] = sum_sig
                    else:
                        for c in range(8):
                            hc = c // 4
                            for half in range(2):
                                b = 2 + half
                                dep0 = [e_qkv, ev.get(f"st{b}")]
                                if half == hc:
                                    r.op("tensor", lambda e, c=c, half=half, b=b, d0=d0: e.matmul(
                                        bk(ps, b), k_sb[d0:d0 + 64, c * 128:(c + 1) * 128],
                                        q_sb[d0:d0 + 64, half * 512:(half + 1) * 512],
                                        start=True, stop=False), deps=dep0, sig=False)
                                    emm = r.op("tensor", lambda e, c=c, b=b: e.matmul(
                                        bk(ps, b)[:, (c % 4) * 128:(c % 4) * 128 + 128],
                                        ident[:], tril8[:], start=False, stop=True), sig=True)
                                else:
                                    emm = r.op("tensor", lambda e, c=c, half=half, b=b, d0=d0: e.matmul(
                                        bk(ps, b), k_sb[d0:d0 + 64, c * 128:(c + 1) * 128],
                                        q_sb[d0:d0 + 64, half * 512:(half + 1) * 512],
                                        start=True, stop=True), deps=dep0, sig=True)
                                bnd = min(max(c * 128 - half * 512, 0), 512)
                                eex = None
                                if bnd > 0:
                                    eex = r.op("scalar", lambda e, half=half, b=b, bnd=bnd: e.activation(
                                        pT[:, half * 512:half * 512 + bnd], bk(ps, b)[:, 0:bnd],
                                        AF.Exp, scale=cvec[:, 2:3], bias=cvec[:, 1:2]), deps=[emm, ev.get("pT_f")])
                                if bnd < 512:
                                    eex = r.op("scalar", lambda e, half=half, b=b, bnd=bnd: e.activation(
                                        pT[:, half * 512 + bnd:(half + 1) * 512], bk(ps, b)[:, bnd:512],
                                        AF.Exp, scale=cvec[:, 2:3]), deps=[emm, ev.get("pT_f")])
                                ev[f"st{b}"] = eex
                                ev[f"expg{half}"] = eex
                            for half in range(2):
                                dep0 = [ev[f"expg{half}"], e_vtm]
                                if c == 0 and half == 0:
                                    dep0 += [ev.get("av_f"), ev.get(f"os{hp}")]
                                av_sig = r.op("tensor", lambda e, c=c, half=half, hp=hp, d0=d0: e.matmul(
                                    ps[d0:d0 + 64, 2048 + half * 512:2048 + (half + 1) * 512] if hp == 0
                                    else ps[64:128, 2048 + half * 512:2048 + (half + 1) * 512],
                                    v_tm[:, c, d0:d0 + 64], pT[:, half * 512:(half + 1) * 512],
                                    start=False, stop=(c == 7)),
                                    deps=dep0 + [e_z], sig=(c == 7))
                                sum_sig = r.op("tensor", lambda e, c=c, half=half, hp=hp, d0=d0, sb_=sum_bks: e.matmul(
                                    ps[d0:d0 + 64, sb_[half] * 512:sb_[half] * 512 + 512],
                                    onesm[:, d0:d0 + 64], pT[:, half * 512:(half + 1) * 512],
                                    start=(c == 0), stop=(c == 7),
                                    skip_group_check=True),
                                    deps=[ev.get(f"os{hp}"), ev.get("pj0"), ev.get("pj1")] if c == 0 else None,
                                    sig=(c == 7))
                            ev["pT_f"] = sum_sig

                    ev[f"os{hp}"] = sum_sig
                    ev[f"erc{hp}"] = sum_sig
                ev["pf0"] = sum_sig
                ev["pf1"] = sum_sig

                eS0 = r.op("vector", lambda e: e.tensor_copy(sums64[0:64, :], ps[0:64, 0:1024]),
                           deps=[ev["os0"], ev.get("sums64_f")])
                eS1 = r.op("vector", lambda e: e.tensor_copy(sums64[64:128, :], ps[64:128, 3072:4096]),
                           deps=[ev["os1"], eS0])
                ev["os0"] = eS0
                ev["os1"] = eS1
                abuf = hg % 2
                eRc = r.op("vector", lambda e: e.reciprocal(acc_f32[:, 0:1024], sums64[:]),
                           deps=[eS1, ev.get("recb_f")])
                eN = r.op("vector", lambda e, abuf=abuf: e.tensor_tensor(
                    atile[:, abuf, :], ps[0:128, 2048:3072], acc_f32[:, 0:1024], ALU.mult),
                    deps=[eRc, ev.get(f"atile_f{abuf}")])
                ev["recb_f"] = eN
                ev["av_f"] = eN
                ev["sums64_f"] = eN
                e_spl = r.dma("sync", lambda e, branch=branch, hg=hg, abuf=abuf: e.dma_start(
                    out=s_att[branch, hg], in_=atile[:, abuf, :]), deps=[eN])
                ev[f"atile_f{abuf}"] = e_spl
                ev["vtm_f"] = e_spl
        fence_att = r.snap()

        e_wo = None
        for kc in range(8):
            e_wo = r.dma("gpsimd", lambda e, kc=kc: e.dma_start(
                out=w_lo[:, kc, :], in_=s_wlo[:, kc, :]), deps=fence_att)
            e_wo = r.dma("gpsimd", lambda e, kc=kc: e.dma_start(
                out=w_go[:, kc, :], in_=s_wgo[:, kc, :]))
        for half in range(2):
            e_as = {}
            for kc in range(16):
                branch, hgk = (0, kc) if kc < 8 else (1, kc - 8)
                e_as[kc] = r.dma("sync", lambda e, branch=branch, hgk=hgk, kc=kc, half=half: e.dma_start(
                    out=astr[:, kc % 2, :], in_=s_att[branch, hgk, :, half * 512:(half + 1) * 512]),
                    deps=fence_att + [ev.get(f"astr_f{kc % 2}")])
                wsel = w_lo if kc < 8 else w_go
                emm8 = None
                for mc in range(8):
                    emm8 = r.op("tensor", lambda e, kc=kc, mc=mc, wsel=wsel, hgk=hgk: e.matmul(
                        bk(ps, mc), wsel[:, hgk, mc * 128:(mc + 1) * 128], astr[:, kc % 2, :],
                        start=(kc == 0), stop=(kc == 15)),
                        deps=[e_as[kc], e_wo] + ([ev.get("op_f")] if kc == 0 else []),
                        sig=(mc == 7))
                ev[f"astr_f{kc % 2}"] = emm8
            e_ops = []
            for mc in range(8):
                eo = r.op("scalar", lambda e, mc=mc, half=half: e.activation(
                    fused_fm[:, mc, half * 512:(half + 1) * 512], bk(ps, mc), AF.Identity,
                    bias=fb8[:, mc:mc + 1]), deps=[emm8, ev["fb8"], ev.get("ax_free")])
                e_ops.append(eo)
            ev["op_f"] = e_ops[-1]
        ev["ax_free"] = e_ops[-1]

        r.op("scalar", lambda e: e.preload_act_table(AF.Sqrt), sig=False)
        e_xs = None
        for dc in range(8):
            e_xs = r.dma("sync", lambda e, dc=dc: e.dma_start(
                out=hid_all[:, :, dc * 128:(dc + 1) * 128],
                in_=fused_fm[:, dc, :], transpose=True),
                deps=[e_ops[-1]] if dc == 0 else None)
        e_st2 = []
        for tb in range(NTILE):
            r.op("scalar", lambda e, tb=tb: e.activation(
                stt_scr[:, 0:1024], hid_all[:, tb, 0:1024], AF.Copy, accum_out=s1c[:, tb:tb + 1]),
                deps=[e_xs] if tb == 0 else None, sig=False)
            e_st2.append(r.op("scalar", lambda e, tb=tb: e.activation(
                stt_scr[:, 1024:2048], hid_all[:, tb, 0:1024], AF.Square, accum_out=s2c[:, tb:tb + 1]),
                drain=(tb == NTILE - 1)))
        eL = r.op("vector", lambda e: e.tensor_scalar(mu_c[:], s1c[:], 1.0 / D, None, ALU.mult), deps=[e_st2[-1]], drain=True)
        eL = r.op("vector", lambda e: e.tensor_copy(mr_c[:], mu_c[:]), deps=[eL], drain=True)
        eL = r.op("vector", lambda e: e.tensor_mul(ln_t1[:], mu_c[:], mr_c[:]), deps=[eL], drain=True)
        eL = r.op("vector", lambda e: e.tensor_scalar(s1c[:], s2c[:], 1.0 / D, None, ALU.mult), deps=[eL], drain=True)
        eL = r.op("vector", lambda e: e.tensor_sub(ln_t1[:], s1c[:], ln_t1[:]), deps=[eL], drain=True)
        eL = r.op("scalar", lambda e: e.activation(s2c[:], ln_t1[:], AF.Sqrt, bias=cvec[:, 0:1]), deps=[eL], drain=True)
        eL = r.op("vector", lambda e: e.reciprocal(rr_c[:], s2c[:]), deps=[eL], drain=True)
        eL = r.op("vector", lambda e: e.tensor_mul(mr_c[:], mu_c[:], rr_c[:]), deps=[eL], drain=True)
        e_ln2 = []
        for tb in range(NTILE):
            eA = r.op("vector", lambda e, tb=tb: e.tensor_scalar(
                k_sb[:, 0:D], hid_all[:, tb, 0:1024], rr_c[:, tb:tb + 1], mr_c[:, tb:tb + 1],
                ALU.mult, ALU.subtract), deps=[eL, ev.get("ksb_f")])
            eA = r.op("vector", lambda e: e.tensor_mul(k_sb[:, 0:D], k_sb[:, 0:D], gam_b[:]), deps=[eA])
            eA = r.op("vector", lambda e: e.tensor_add(k_sb[:, 0:D], k_sb[:, 0:D], bet_b[:]), deps=[eA])
            esx = r.dma("sync", lambda e, tb=tb: e.dma_start(out=s_stm[:, tb, :], in_=k_sb[:, 0:D]), deps=[eA])
            ev["ksb_f"] = esx
            e_ln2.append(esx)
        fence_ln2 = r.snap()

        e_wf = None
        for kc in range(8):
            e_wf = r.dma("gpsimd", lambda e, kc=kc: e.dma_start(
                out=w_win[:, kc, :], in_=d_win[kc * 128:(kc + 1) * 128, :]), deps=fence_ln2)
        e_mtl = r.dma("gpsimd", lambda e: e.dma_start(out=mt_sb[:], in_=s_mt[:]), deps=fence_ln2)
        r.op("scalar", lambda e: e.preload_act_table(AF.Gelu), sig=False)
        sA = {}

        def stage1A(tb):
            gb = tb % 2
            e_xnr = r.dma("sync", lambda e, tb=tb, gb=gb: e.dma_start(
                out=xn_tmg[:, gb, :], in_=s_xntm[:, tb, :]), deps=[ev.get(f"xnr_f{gb}")] + fence_ln2)
            e_sim = None
            for kc in range(8):
                for nb in range(4):
                    e_sim = r.op("tensor", lambda e, tb=tb, kc=kc, nb=nb: e.matmul(
                        bk(ps, nb), xn_fm[:, kc, tb * 128:(tb + 1) * 128],
                        mt_sb[:, kc, nb * 512:(nb + 1) * 512],
                        start=(kc == 0), stop=(kc == 7)),
                        deps=([e_mtl, ev.get("simf")] + fence_ln2) if (kc == 0 and nb == 0) else None,
                        sig=(kc == 7 and nb == 3))
            e_mx = None
            for ph in range(16):
                nb, col = ph // 4, (ph % 4) * 128
                r.op("vector", lambda e, ph=ph, nb=nb, col=col: e.max(
                    out=mx8[:, ph, :], in_=bk(ps, nb)[:, col:col + 128]),
                    deps=[e_sim] if ph == 0 else None, sig=False)
            for ph in range(16):
                nb, col = ph // 4, (ph % 4) * 128
                e_mx = r.op("vector", lambda e, ph=ph, nb=nb, col=col: e.max_index(
                    mi_u[:, ph, :], mx8[:, ph, :], bk(ps, nb)[:, col:col + 128]), deps=None,
                    sig=(ph == 15))
            ev["simf"] = e_mx
            eg = r.op("vector", lambda e: e.tensor_add(
                score8[:], mx8[:, 0:8, 0], mx8[:, 8:16, 0]), deps=[e_mx], drain=True)
            eg = r.op("vector", lambda e, tb=tb: e.tensor_scalar_max(
                gate_all[:, tb, :], score8[:], 0.0), deps=[eg], drain=True)
            ee = r.op("vector", lambda e, tb=tb: e.tensor_scalar(
                e_all[:, tb, :], mi_u[:, 0:8, 0], 128.0, None, ALU.mult), deps=[e_mx], drain=True)
            ee = r.op("vector", lambda e, tb=tb: e.tensor_add(
                e_all[:, tb, :], e_all[:, tb, :], mi_u[:, 8:16, 0]), deps=[ee], drain=True)
            ev[f"e_done{tb}"] = ee
            sA[tb] = (e_xnr, eg, ee)

        pendA = []

        def issueA(tb, h):
            ee = sA[tb][2]
            sl = r.gctr % 4
            r.gctr += 1
            r.dma("gpsimd", lambda e, tb=tb, h=h, sl=sl: e.indirect_dma_start(
                out=gbuf2[:, sl, 0:1024], out_offset=None, in_=d_ina[:],
                in_offset=IndirectOffsetOnAxis(ap=e_all[:, tb, h:h + 1], axis=0)),
                deps=[ee, ev.get(f"gbf{sl}")])
            egt = r.dma("gpsimd", lambda e, tb=tb, h=h, sl=sl: e.indirect_dma_start(
                out=gbuf2[:, sl, 1024:3072], out_offset=None, in_=d_inb[:],
                in_offset=IndirectOffsetOnAxis(ap=e_all[:, tb, h:h + 1], axis=0)),
                deps=None)
            pendA.append((sl, egt))

        def stage2A(tb):
            gb = tb % 2
            e_xnr, eg, ee = sA[tb]
            e_wm = None
            for kc in range(8):
                for nb in range(4):
                    e_wm = r.op("tensor", lambda e, tb=tb, kc=kc, nb=nb: e.matmul(
                        bk(ps, 4 + nb), xn_fm[:, kc, tb * 128:(tb + 1) * 128],
                        w_win[:, kc, nb * 512:(nb + 1) * 512],
                        start=(kc == 0), stop=(kc == 7)),
                        deps=[e_wf, ev.get("hidf")] if (kc == 0 and nb == 0) else None,
                        sig=(kc == 7 and nb == 3))
            e_acc = r.op("vector", lambda e: e.tensor_copy(acc_f32[:], ps[0:128, 2048:4096]),
                         deps=[e_wm, ev.get("accf")])
            ev["hidf"] = e_acc
            e_corr = e_acc
            ed = None
            for h in range(8):
                sl, egt = pendA.pop(0)
                ed = r.op("vector", lambda e, tb=tb, h=h, sl=sl: e.scalar_tensor_tensor(
                    out=stt_scr[:, 0:1024], in0=gbuf2[:, sl, 0:1024],
                    scalar=gate_all[:, tb, h:h + 1], in1=xn_tmg[:, tb % 2, :],
                    op0=ALU.mult, op1=ALU.mult, accum_out=lingc[:, h:h + 1]),
                    deps=[egt, e_xnr, eg], drain=True)
                e_corr = r.op("vector", lambda e, h=h, sl=sl: e.scalar_tensor_tensor(
                    out=acc_f32[:], in0=gbuf2[:, sl, 1024:3072],
                    scalar=lingc[:, h:h + 1], in1=acc_f32[:], op0=ALU.mult, op1=ALU.add),
                    deps=[ed, e_corr])
                ev[f"gbf{sl}"] = e_corr
                if h < 4:
                    issueA(tb, h + 4)
                elif tb + 1 < NTILE:
                    issueA(tb + 1, h - 4)
            e_gel = r.op("scalar", lambda e, tb=tb: e.activation(
                hid_all[:, tb, :], acc_f32[:], AF.Gelu), deps=[e_corr])
            ev["accf"] = e_gel
            ev[f"xnr_f{gb}"] = ed
            ev[f"gelu{tb}"] = e_gel

        for tb in range(NTILE):
            stage1A(tb)
            if tb == 0:
                for h in range(4):
                    issueA(0, h)
            if tb >= 1:
                stage2A(tb - 1)
        stage2A(NTILE - 1)
        if dbg:
            r.dma("sync", lambda e: e.dma_start(out=s_hid[:], in_=hid_all[:]), deps=r.snap())
            r.dma("sync", lambda e: e.dma_start(out=s_edbg[:], in_=e_all[:]))
            r.dma("sync", lambda e: e.dma_start(out=s_gdbg[:], in_=gate_all[:]))
            r.dma("sync", lambda e: e.dma_start(out=s_mxdbg[:], in_=mx8[:]))
        fence_ffna = r.snap()

        e_wf2 = None
        for kc in range(16):
            e_wf2 = r.dma("gpsimd", lambda e, kc=kc: e.dma_start(
                out=w_wout[:, kc, :], in_=d_wout[kc * 128:(kc + 1) * 128, :]), deps=fence_ffna)
        r.op("scalar", lambda e: e.preload_act_table(AF.Silu), sig=False)
        sB = {}

        def stage1B(tb):
            gb = tb % 2
            e_str = r.dma("sync", lambda e, tb=tb, gb=gb: e.dma_start(
                out=s_tmg[:, gb, :], in_=s_stm[:, tb, :]), deps=[ev.get(f"str_f{gb}")] + fence_ffna)
            e_x2 = r.dma("sync", lambda e, tb=tb, gb=gb, t0=t0: e.dma_start(
                out=x2_t[:, gb, :], in_=dx[t0 + tb * 128:t0 + tb * 128 + 128, :]),
                deps=[ev.get(f"x2_f{gb}")])
            e_hfm = r.dma("sync", lambda e, tb=tb, gb=gb: e.dma_start(
                out=hid_fm[:, gb], in_=hid_all[:, tb, :], transpose=True),
                deps=[ev[f"gelu{tb}"], ev.get(f"hfm_f{gb}")])
            e_wo2 = None
            for kc in range(16):
                for nb in range(2):
                    e_wo2 = r.op("tensor", lambda e, tb=tb, gb=gb, kc=kc, nb=nb: e.matmul(
                        bk(ps, nb), hid_fm[:, gb, kc, :], w_wout[:, kc, nb * 512:(nb + 1) * 512],
                        start=(kc == 0), stop=(kc == 15)),
                        deps=[e_wf2, e_hfm, ev.get("outf")] if (kc == 0 and nb == 0) else None,
                        sig=(kc == 15 and nb == 1))
            ev[f"hfm_f{gb}"] = e_wo2
            e_oacc = r.op("vector", lambda e, gb=gb: e.tensor_copy(
                acc_f32[:, gb * 1024:(gb + 1) * 1024], ps[0:128, 0:1024]),
                deps=[e_wo2, ev.get(f"bacc_f{gb}")])
            ev["outf"] = e_oacc
            sB[tb] = (e_str, e_x2, e_oacc, e_wo2)

        pendB = []

        def issueB(tb, h):
            sl = r.gctr % 4
            r.gctr += 1
            r.dma("gpsimd", lambda e, tb=tb, h=h, sl=sl: e.indirect_dma_start(
                out=gbuf2[:, sl, 0:2048], out_offset=None, in_=d_outa[:],
                in_offset=IndirectOffsetOnAxis(ap=e_all[:, tb, h:h + 1], axis=0)),
                deps=[ev[f"e_done{tb}"], ev.get(f"gbf{sl}")])
            egt = r.dma("gpsimd", lambda e, tb=tb, h=h, sl=sl: e.indirect_dma_start(
                out=gbuf2[:, sl, 2048:3072], out_offset=None, in_=d_outb[:],
                in_offset=IndirectOffsetOnAxis(ap=e_all[:, tb, h:h + 1], axis=0)),
                deps=None)
            pendB.append((sl, egt))

        def stage2B(tb):
            gb = tb % 2
            e_str, e_x2, e_oacc, e_wo2 = sB[tb]
            e_corr = e_oacc
            for h in range(8):
                sl, egt = pendB.pop(0)
                ed = r.op("vector", lambda e, tb=tb, h=h, sl=sl: e.scalar_tensor_tensor(
                    out=stt_scr[:, 0:2048], in0=gbuf2[:, sl, 0:2048],
                    scalar=gate_all[:, tb, h:h + 1], in1=hid_all[:, tb, :],
                    op0=ALU.mult, op1=ALU.mult, accum_out=lingc[:, h:h + 1]),
                    deps=[egt, ev[f"gelu{tb}"]], drain=True)
                e_corr = r.op("vector", lambda e, h=h, sl=sl, gb=gb: e.scalar_tensor_tensor(
                    out=acc_f32[:, gb * 1024:(gb + 1) * 1024], in0=gbuf2[:, sl, 2048:3072],
                    scalar=lingc[:, h:h + 1], in1=acc_f32[:, gb * 1024:(gb + 1) * 1024],
                    op0=ALU.mult, op1=ALU.add),
                    deps=[ed, e_corr])
                ev[f"gbf{sl}"] = e_corr
                if h < 4:
                    issueB(tb, h + 4)
                elif tb + 1 < NTILE:
                    issueB(tb + 1, h - 4)
            e_sil = r.op("scalar", lambda e, gb=gb: e.activation(
                g_t[:, gb, :], acc_f32[:, gb * 1024:(gb + 1) * 1024], AF.Silu),
                deps=[e_corr, ev.get(f"gt_f{gb}")])
            e_p = r.op("vector", lambda e, gb=gb: e.tensor_mul(
                acc_f32[:, gb * 1024:(gb + 1) * 1024], g_t[:, gb, :], s_tmg[:, gb, :]), deps=[e_sil, e_str])
            ev[f"gt_f{gb}"] = e_p
            ev[f"str_f{gb}"] = e_p
            e_o = r.op("vector", lambda e, gb=gb: e.tensor_add(
                x2_t[:, gb, :], x2_t[:, gb, :], acc_f32[:, gb * 1024:(gb + 1) * 1024]),
                deps=[e_p, e_x2, ev.get(f"out_f{gb}")])
            e_wr = r.dma("sync", lambda e, tb=tb, gb=gb, t0=t0: e.dma_start(
                out=d_out[t0 + tb * 128:t0 + tb * 128 + 128, :], in_=x2_t[:, gb, :]), deps=[e_o])
            ev[f"x2_f{gb}"] = e_wr
            ev[f"out_f{gb}"] = e_wr
            ev[f"bacc_f{gb}"] = e_o
            return e_wr, e_wo2

        for tb in range(NTILE):
            stage1B(tb)
            if tb == 0:
                for h in range(4):
                    issueB(0, h)
            if tb >= 1:
                e_out_final, e_wo2 = stage2B(tb - 1)
        e_out_final, e_wo2 = stage2B(NTILE - 1)
        ev["arena_w_free"] = e_wo2

    r.wait("sync", [e_out_final])
    r.emit()
    return nc



def _prep_inputs(inp):
    x = np.ascontiguousarray(inp["L0_attn_x"])
    common = {
        "w_pl": np.ascontiguousarray(inp["L0_attn_proj_local"]),
        "w_pg": np.ascontiguousarray(inp["L0_attn_proj_global"]),
        "w_l_inT": np.ascontiguousarray(inp["L0_attn_Wl_in"].T),
        "w_g_inT": np.ascontiguousarray(inp["L0_attn_Wg_in"].T),
        "b_l_in3": np.ascontiguousarray(inp["L0_attn_bl_in"].reshape(24, 128).T),
        "b_g_in3": np.ascontiguousarray(inp["L0_attn_bg_in"].reshape(24, 128).T),
        "w_l_outT": np.ascontiguousarray(inp["L0_attn_Wl_out"].T),
        "w_g_outT": np.ascontiguousarray(inp["L0_attn_Wg_out"].T),
        "b_l_out8": np.ascontiguousarray(inp["L0_attn_bl_out"].reshape(8, 128).T),
        "b_g_out8": np.ascontiguousarray(inp["L0_attn_bg_out"].reshape(8, 128).T),
        "fusion_w": np.ascontiguousarray(inp["L0_attn_fusion_w"].reshape(1, 2)),
        "gamma_b": np.ascontiguousarray(np.broadcast_to(inp["L0_attn_ln_gamma"], (128, D))),
        "beta_b": np.ascontiguousarray(np.broadcast_to(inp["L0_attn_ln_beta"], (128, D))),
        "wq": np.ascontiguousarray(inp["L0_ffn_Wq"]),
        "keysT": np.ascontiguousarray(inp["L0_ffn_keys"].transpose(2, 0, 3, 1).reshape(16, 512, 128)),
        "w_inT": np.ascontiguousarray(inp["L0_ffn_W_in"].T),
        "w_outT": np.ascontiguousarray(inp["L0_ffn_W_out"].T),
        "tab_ina": np.ascontiguousarray(inp["L0_ffn_in_a"]),
        "tab_inb": np.ascontiguousarray(inp["L0_ffn_in_b"]),
        "tab_outa": np.ascontiguousarray(inp["L0_ffn_out_a"]),
        "tab_outb": np.ascontiguousarray(inp["L0_ffn_out_b"]),
        "ident_bf": np.eye(128, dtype=BF),
        "tril8_bf": (8.0 * np.tril(np.ones((128, 128), np.float32), k=-1)).astype(BF),
        "onesrow": np.ones((1, 128), np.float32),
        "onescol": np.ones((128, 1), BF),
        "onesmat": np.ones((128, 128), BF),
        "onesrow_bf": np.ones((1, 128), BF),
        "zerocol": np.zeros((128, 128), BF),
        "constv": np.ascontiguousarray(np.broadcast_to(
            np.array([1e-5, 1.0, 0.125, 0.0], np.float32), (128, 4))),
    }
    in_maps = []
    for c in range(NCORES):
        b, q0 = c // 4, (c % 4) * T
        m = dict(common)
        m["x_rows"] = np.ascontiguousarray(x[b, q0:q0 + T, :])
        in_maps.append(m)
    return in_maps


def kernel(**inputs):
    in_maps = _prep_inputs(inputs)
    nc = build_program()
    res = run_bass_kernel_spmd(nc, in_maps, core_ids=list(range(NCORES)))
    out = np.zeros((2, 4 * T, D), np.float32)
    for c in range(NCORES):
        b, q0 = c // 4, (c % 4) * T
        out[b, q0:q0 + T, :] = res.results[c]["out_rows"]
    return out
